# version 03 (final)
# baseline (speedup 1.0000x reference)
import contextlib
import numpy as np
import concourse.bass as bass
import concourse.mybir as mybir

F32 = mybir.dt.float32
BF16 = mybir.dt.bfloat16
I16 = mybir.dt.int16
U16 = mybir.dt.uint16
AF = mybir.ActivationFunctionType
ALU = mybir.AluOpType
AX = mybir.AxisListType

P = 128
T = 4096
SEQ = 2048
NT = 512
NTILES = T // NT
DM = 1024
KC = DM // P
INNER = 2048
GC = INNER // P
EPS = 1e-5
MAGIC = 12582912.0

ALL_PHASES = ["A0", "A1", "A2", "B", "D", "E", "F", "G", "H"]


class St:
    pass


_NOPPABLE = {"matmul", "transpose", "tensor_tensor", "tensor_copy",
             "tensor_scalar", "tensor_single_scalar", "tensor_reduce",
             "tensor_tensor_scan", "activation", "memset", "iota",
             "reciprocal", "scalar_tensor_tensor"}


class EngProxy:
    def __init__(self, e, st, eng_name=None):
        self._e = e
        self._st = st
        self._eng = eng_name

    def wait_ge(self, h, v):
        return self._e.wait_ge(h, v + self._st.base.get(h.num, 0))

    def __getattr__(self, n):
        if (n in _NOPPABLE and self._st.nop_active
                and self._eng in getattr(self._st, "e_nop", ())):
            return lambda *a, **kw: self._e.nop()
        return getattr(self._e, n)


class BlockProxy:
    def __init__(self, b, st):
        self._b = b
        self._st = st

    def __getattr__(self, eng):
        deco = getattr(self._b, eng)

        def mydeco(f):
            return deco(lambda e: f(EngProxy(e, self._st, eng)))
        return mydeco



def build(cfg):
    taps = set(cfg.get("taps", ()))
    phases = cfg.get("phases", ALL_PHASES)
    st = St()
    st.phases = phases
    st.e_nch = cfg.get("e_nch", SEQ // 64)
    st.e_dis = set(cfg.get("e_dis", ()))
    st.e_nop = set(cfg.get("e_nop", ()))
    st.nop_active = False
    nc = bass.Bass("TRN2", target_bir_lowering=False)
    st.nc = nc

    def dram(name, shape, dtype):
        kind = "ExternalOutput" if name in taps else "Internal"
        return nc.dram_tensor(name, shape, dtype, kind=kind).ap()

    def din(name, shape, dtype=F32):
        return nc.dram_tensor(name, shape, dtype, kind="ExternalInput").ap()

    st.x_fm = din("x_fm", [DM, T])
    st.w_in = din("w_in", [DM, 2 * INNER])
    st.b_in = din("b_in", [P, 32])
    st.convw = din("convw", [P, GC * 3])
    st.convb = din("convb", [P, GC])
    st.w_dbc = din("w_dbc", [INNER, 192])
    st.b_dbc = din("b_dbc", [64, 3])
    st.w_si = din("w_si", [64, INNER])
    st.b_si_row = din("b_si_row", [1, INNER])
    st.b_out0_row = din("b_out0_row", [1, DM])
    st.b_si = din("b_si", [P, GC])
    st.d0 = din("d0", [P, GC])
    st.w_out0 = din("w_out0", [INNER, DM])
    st.b_out0 = din("b_out0", [P, KC])
    st.w_rkv = din("w_rkv", [DM, 3 * DM])
    st.mu_rk = din("mu_rk", [P, 2 * KC])
    st.w_in1 = din("w_in1", [DM, 1184])
    st.dtb1 = din("dtb1", [32, 1])
    st.alog1 = din("alog1", [32, 1])
    st.d1 = din("d1", [32, 1])
    st.w_out1 = din("w_out1", [DM, DM])
    st.w_up_t = din("w_up_t", [DM, 3072])
    st.w_gate_t = din("w_gate_t", [DM, 3072])
    st.w_down_t = din("w_down_t", [3072, DM])
    st.w_up_n = din("w_up_n", [3072, DM])
    st.w_gate_n = din("w_gate_n", [3072, DM])
    st.w_down_n = din("w_down_n", [DM, 3072])

    st.xn0_d = dram("xn0", [DM, T], BF16)
    st.xmc_d = dram("xmc", [INNER, T], BF16)
    st.gate0_d = dram("gate0", [INNER, T], BF16)
    st.xprime_d = dram("xprime", [DM, T], F32)
    st.x1_d = dram("x1", [DM, T], F32)
    st.xn1_d = dram("xn1", [DM, T], BF16)
    st.xbcdt_d = dram("xbcdt", [1184, T], BF16)
    st.up_d = dram("up_raw", [3072, T], BF16)
    st.gate1_d = dram("gate_raw", [3072, T], BF16)
    st.o_d = dram("o_raw", [3072, T], BF16)
    st.dtpre_d = dram("dtpre", [32, T], F32)
    st.ea_d = nc.dram_tensor("ea_bnc", [32, SEQ], F32).ap()
    st.out_d = dram("out", [DM, T], F32)
    st.stb_d = nc.dram_tensor("stb", [8, T], F32).ap()
    st.mb_d = nc.dram_tensor("mbounce", [16, P], F32).ap()
    st.scl_d = dram("scl", [1, 32], F32)
    st.cc_in_d = nc.dram_tensor("cc_in", [1, 16], F32).ap()
    st.cc_out_ds = [nc.dram_tensor(f"cc_out{i}", [1, 16], F32, addr_space="Shared").ap()
                    for i in range(4)]

    with contextlib.ExitStack() as es:
        E = es.enter_context
        st.E = E
        st.sems = {}
        st.pool = []
        st.phase_map = {}
        GLOBAL_SEMS = {"ph", "c0", "eb"}
        DMA_SEM_NAMES = {"a0g", "a1w", "a1r", "a2w", "b1w", "b1g", "b2w", "b2r",
                         "b2gg", "d1w", "d1r", "d1db", "dq_up", "dq_ga", "dq_dn",
                         "d2acc", "d2fl", "d2db", "est_cp", "f_db", "g_db", "h_st"}
        st.eb_count = 0

        def sem(name):
            if name in GLOBAL_SEMS or name in DMA_SEM_NAMES:
                if name not in st.sems:
                    st.sems[name] = E(nc.semaphore(name))
                return st.sems[name]
            if name not in st.phase_map:
                slot = len(st.phase_map)
                while slot >= len(st.pool):
                    st.pool.append(E(nc.semaphore(f"pool{len(st.pool)}")))
                st.phase_map[name] = slot
            return st.pool[st.phase_map[name]]
        st.sem = sem

        st.ones_bf = E(nc.sbuf_tensor("ones_bf", [P, P], BF16))
        st.ones_f = E(nc.sbuf_tensor("ones_f", [1, P], F32))
        st.ident_bf = E(nc.sbuf_tensor("ident_bf", [P, P], BF16))
        st.iota_f = E(nc.sbuf_tensor("iota_f", [P, P], F32))
        st.epscol = E(nc.sbuf_tensor("epscol", [P, 1], F32))
        st.onecol = E(nc.sbuf_tensor("onecol", [P, 1], F32))
        st.c327 = E(nc.sbuf_tensor("c327", [1, 2], F32))
        st.ident_f = E(nc.sbuf_tensor("ident_f", [P, P], F32))
        st.c7col = E(nc.sbuf_tensor("c7col", [P, 2], F32))
        st.ph = sem("ph")
        st.phase_idx = 0

        block = BlockProxy(E(nc.Block()), st)
        st.block = block
        st.base = {}

        @block.gpsimd
        def _(g):
            g.iota(st.iota_f[:], pattern=[[1, P]], base=0, channel_multiplier=-1,
                   allow_small_or_imprecise_dtypes=True).then_inc(sem("c0"), 1)

        @block.vector
        def _(v):
            v.wait_ge(sem("c0"), 1)
            v.memset(st.ones_bf[:], 1.0)
            v.memset(st.ones_f[:], 1.0)
            v.tensor_single_scalar(st.ident_bf[:], st.iota_f[:], 0.0, ALU.is_equal)
            v.memset(st.epscol[:], EPS)
            v.memset(st.onecol[:], 1.0)
            v.tensor_single_scalar(st.ident_f[:], st.iota_f[:], 0.0, ALU.is_equal)
            v.memset(st.c7col[:, 0:1], 7.0)
            v.drain()
            v.reciprocal(st.c7col[:, 1:2], st.c7col[:, 0:1])
            v.drain()
            v.memset(st.c327[0:1, 0:1], 32767.0)
            v.drain()
            v.reciprocal(st.c327[0:1, 1:2], st.c327[0:1, 0:1])
            v.drain()
            v.nop().then_inc(st.ph, 1)
        st.phase_idx = 1

        if "A0" in phases:
            phase_a0(st)
        if "A1" in phases:
            phase_a1(st)
        if "A2" in phases:
            phase_a2(st)
        if "B" in phases:
            phase_b(st)
        if "D" in phases:
            phase_d(st)
        if "E" in phases:
            st.nop_active = True
            phase_e(st)
            st.nop_active = False
        if "F" in phases:
            phase_f(st)
        if "G" in phases:
            phase_g(st)
        if "H" in phases:
            phase_h(st)

    return nc


def barrier(st, waits):
    import collections
    block, sem = st.block, st.sem
    totals = collections.Counter()
    insts = list(st.nc.all_instructions())
    for i in insts[getattr(st, "inst_mark", 0):]:
        si = getattr(i, "sync_info", None)
        if si is None:
            continue
        for u in si.on_update:
            if u.sync_type == "semaphore":
                totals[u.id] += u.update_value
    pool_by_id = {h.num: h for h in st.pool}

    @block.gpsimd
    def _(g):
        for s, c in waits:
            g.wait_ge(s, c)
        for sid, tot in sorted(totals.items()):
            if sid in pool_by_id and tot > 0:
                g.wait_ge(pool_by_id[sid], tot)
        g.nop().then_inc(st.ph, 1)
    st.phase_idx += 1
    st.phase_map = {}
    st.inst_mark = len(list(st.nc.all_instructions()))
    global_ids = {h.num for h in st.sems.values()}
    for sid, tot in totals.items():
        if sid not in global_ids:
            st.base[sid] = st.base.get(sid, 0) + tot


def phase_a0(st):
    nc, block, E, sem = st.nc, st.block, st.E, st.sem
    PH = st.phase_idx
    a0x = sem("a0x"); a0v = sem("a0v"); a0m = sem("a0m"); a0f = sem("a0f")
    with contextlib.ExitStack() as ps:
        PE = ps.enter_context
        xin = [PE(nc.sbuf_tensor(f"a0_x{i}", [P, KC, NT], F32)) for i in range(2)]
        xbf = PE(nc.sbuf_tensor("a0_xbf", [P, KC, NT], BF16))
        xsq = PE(nc.sbuf_tensor("a0_xsq", [P, KC, NT], BF16))
        mrow = PE(nc.sbuf_tensor("a0_mrow", [1, T], F32))
        qrow = PE(nc.sbuf_tensor("a0_qrow", [1, T], F32))
        mg = PE(nc.sbuf_tensor("a0_mg", [P, T // P], F32))
        qg = PE(nc.sbuf_tensor("a0_qg", [P, T // P], F32))
        t1 = PE(nc.sbuf_tensor("a0_t1", [P, T // P], F32))
        rg = PE(nc.sbuf_tensor("a0_rg", [P, T // P], F32))
        lnv0 = PE(nc.sbuf_tensor("a0_lnv", [P, T // P], F32))
        ng = PE(nc.sbuf_tensor("a0_ng", [P, T // P], F32))
        psum = PE(nc.psum_tensor("a0_ps", [P, NT], F32))
        psq = PE(nc.psum_tensor("a0_ps2", [P, NT], F32))

        x_t = st.x_fm.rearrange("(g q) t -> q g t", q=P)

        @block.sync
        def _(sync):
            sync.wait_ge(st.ph, PH)
            for t in range(NTILES):
                if t >= 1:
                    sync.wait_ge(a0x, 16 * t)
                if t >= 2:
                    sync.wait_ge(a0v, t - 1)
                sync.dma_start(out=xin[t % 2][:, :, :],
                               in_=x_t[:, :, t * NT:(t + 1) * NT]).then_inc(a0x, 16)

        @block.vector
        def _(v):
            v.wait_ge(st.ph, PH)
            for t in range(NTILES):
                v.wait_ge(a0x, 16 * (t + 1))
                v.drain()
                for g in range(KC):
                    inst = v.tensor_copy(xbf[:, g, :], xin[t % 2][:, g, :])
                inst.then_inc(a0v, 1)
                v.wait_ge(a0m, 2 * (t + 1))
                v.tensor_scalar(mrow[0:1, t * NT:(t + 1) * NT], psum[0:1, :],
                                1.0 / DM, 0.0, ALU.mult, ALU.add)
                v.tensor_scalar(qrow[0:1, t * NT:(t + 1) * NT], psq[0:1, :],
                                1.0 / DM, 0.0, ALU.mult, ALU.add).then_inc(a0f, 1)
            v.wait_ge(sem("a0g"), 64)
            v.tensor_tensor(t1[:], mg[:], mg[:], ALU.mult)
            v.drain()
            v.tensor_tensor(qg[:], qg[:], t1[:], ALU.subtract).then_inc(a0f, 1)

        @block.scalar
        def _(s):
            s.wait_ge(st.ph, PH)
            for t in range(NTILES):
                s.wait_ge(a0v, t + 1)
                for g in range(KC):
                    inst = s.activation(xsq[:, g, :], xbf[:, g, :], AF.Square)
                inst.then_inc(sem("a0s"), 1)
            s.wait_ge(a0f, NTILES + 1)
            s.activation(lnv0[:], qg[:], AF.Ln, bias=st.epscol[:, 0:1], scale=1.0)
            s.drain()
            s.activation(rg[:], lnv0[:], AF.Exp, bias=0.0, scale=-0.5).then_inc(sem("a0s"), 1)

        @block.vector
        def _(v):
            v.wait_ge(sem("a0s"), NTILES + 1)
            v.tensor_tensor(ng[:], mg[:], rg[:], ALU.mult).then_inc(a0f, 1)

        @block.tensor
        def _(tns):
            tns.wait_ge(st.ph, PH)
            for t in range(NTILES):
                tns.wait_ge(sem("a0s"), t + 1)
                if t >= 1:
                    tns.wait_ge(a0f, t)
                for g in range(KC):
                    inst = tns.matmul(psum[:, :], st.ones_bf[:], xbf[:, g, :],
                                      start=(g == 0), stop=(g == KC - 1))
                inst.then_inc(a0m, 1)
                for g in range(KC):
                    inst = tns.matmul(psq[:, :], st.ones_bf[:], xsq[:, g, :],
                                      start=(g == 0), stop=(g == KC - 1))
                inst.then_inc(a0m, 1)

        @block.gpsimd
        def _(g):
            g.wait_ge(st.ph, PH)
            g.wait_ge(a0f, NTILES)
            g.dma_start(out=st.stb_d[0:1, :], in_=mrow[0:1, :]).then_inc(sem("a0g"), 16)
            g.wait_ge(sem("a0g"), 16)
            g.dma_start(out=st.stb_d[1:2, :], in_=qrow[0:1, :]).then_inc(sem("a0g"), 16)
            g.wait_ge(sem("a0g"), 32)
            g.dma_start(out=mg[:, :],
                        in_=st.stb_d[0, :].rearrange("(p c) -> p c", p=P)).then_inc(sem("a0g"), 16)
            g.wait_ge(sem("a0g"), 48)
            g.dma_start(out=qg[:, :],
                        in_=st.stb_d[1, :].rearrange("(p c) -> p c", p=P)).then_inc(sem("a0g"), 16)
            g.wait_ge(a0f, NTILES + 2)
            g.dma_start(out=st.stb_d[2, :].rearrange("(p c) -> p c", p=P),
                        in_=rg[:, :]).then_inc(sem("a0g"), 16)
            g.wait_ge(sem("a0g"), 80)
            g.dma_start(out=st.stb_d[3, :].rearrange("(p c) -> p c", p=P),
                        in_=ng[:, :]).then_inc(sem("a0g"), 16)


        barrier(st, [(sem("a0g"), 96)])


def phase_a1(st):
    nc, block, E, sem = st.nc, st.block, st.E, st.sem
    PH = st.phase_idx
    a1x = sem("a1x"); a1w = sem("a1w"); a1bc = sem("a1bc"); a1ap = sem("a1ap")
    a1mm = sem("a1mm"); a1ev = sem("a1ev"); a1cv = sem("a1cv")
    s_xn = sem("a1s_xn"); s_xmc = sem("a1s_xmc"); s_gt = sem("a1s_gt")
    OC = 32
    with contextlib.ExitStack() as ps:
        PE = ps.enter_context
        w_sb = PE(nc.sbuf_tensor("a1_w", [P, KC * 2 * INNER // P * P], BF16))
        bcol = PE(nc.sbuf_tensor("a1_b", [P, OC], F32))
        cw = PE(nc.sbuf_tensor("a1_cw", [P, GC * 3], F32))
        cb = PE(nc.sbuf_tensor("a1_cb", [P, GC], F32))
        xin = [PE(nc.sbuf_tensor("a1_x0", [P, KC, NT], F32))] * 2
        xnb = [PE(nc.sbuf_tensor(f"a1_xn{i}", [P, KC, NT], BF16)) for i in range(2)]
        rrow = [PE(nc.sbuf_tensor(f"a1_rr{i}", [1, NT], F32)) for i in range(2)]
        nrow = [PE(nc.sbuf_tensor(f"a1_nr{i}", [1, NT], F32)) for i in range(2)]
        xmp = [PE(nc.sbuf_tensor(f"a1_xmp{i}", [P, GC, NT + 2], BF16)) for i in range(2)]
        c1 = PE(nc.sbuf_tensor("a1_c1", [P, NT], F32))
        tf = c1
        c2 = PE(nc.sbuf_tensor("a1_c2", [P, NT], F32))
        c3 = PE(nc.sbuf_tensor("a1_c3", [P, 2, NT], F32))
        xmc_sb = [PE(nc.sbuf_tensor(f"a1_xmc{i}", [P, GC, NT], BF16)) for i in range(2)]
        gt_sb = [PE(nc.sbuf_tensor("a1_gt0", [P, GC, NT], BF16))] * 2
        pmm = [PE(nc.psum_tensor(f"a1_p{i}", [P, NT], F32)) for i in range(2)]
        pr = [PE(nc.psum_tensor(f"a1_pr{i}", [P, NT], F32)) for i in range(2)]
        pn = [PE(nc.psum_tensor(f"a1_pn{i}", [P, NT], F32)) for i in range(2)]

        def wsl(k, o):
            return w_sb[:, (k * OC + o) * P:(k * OC + o) * P + P]

        x_t = st.x_fm.rearrange("(g q) t -> q g t", q=P)
        xn0_t = st.xn0_d.rearrange("(g q) t -> q g t", q=P)
        xmc_t = st.xmc_d.rearrange("(g q) t -> q g t", q=P)
        gt_t = st.gate0_d.rearrange("(g q) t -> q g t", q=P)

        @block.gpsimd
        def _(g):
            g.wait_ge(st.ph, PH)
            nw = 0
            for k in range(KC):
                if nw:
                    g.wait_ge(a1w, nw)
                g.dma_start(out=w_sb[:, k * 4096:(k + 1) * 4096],
                            in_=st.w_in[k * P:(k + 1) * P, :]).then_inc(a1w, 16)
                nw += 16
            for (dst, srcq) in [(bcol, st.b_in), (cw, st.convw), (cb, st.convb)]:
                g.wait_ge(a1w, nw)
                g.dma_start(out=dst[:, :], in_=srcq[:, :]).then_inc(a1w, 16)
                nw += 16
            for t in range(NTILES):
                if t >= 1:
                    g.wait_ge(sem("a1r"), 32 * t)
                if t >= 2:
                    g.wait_ge(a1bc, t - 1)
                g.dma_start(out=rrow[t % 2][:, :],
                            in_=st.stb_d[2, t * NT:(t + 1) * NT].unsqueeze(0)).then_inc(sem("a1r"), 16)
                g.wait_ge(sem("a1r"), 32 * t + 16)
                g.dma_start(out=nrow[t % 2][:, :],
                            in_=st.stb_d[3, t * NT:(t + 1) * NT].unsqueeze(0)).then_inc(sem("a1r"), 16)
        NW = 16 * (KC + 3)

        @block.sync
        def _(sync):
            sync.wait_ge(st.ph, PH)
            for t in range(NTILES):
                if t >= 1:
                    sync.wait_ge(a1x, 16 * t)
                    sync.wait_ge(a1ap, t)
                sync.dma_start(out=xin[t % 2][:, :, :],
                               in_=x_t[:, :, t * NT:(t + 1) * NT]).then_inc(a1x, 16)

        @block.tensor
        def _(tns):
            tns.wait_ge(st.ph, PH)
            tns.wait_ge(a1w, NW)
            for t in range(NTILES):
                b = t % 2
                if t >= 2:
                    tns.wait_ge(a1ap, t - 1)
                tns.wait_ge(sem("a1r"), 32 * (t + 1))
                tns.matmul(pr[b][:, :], st.ones_f[0:1, :],
                           rrow[b][0:1, :], start=True, stop=True)
                inst = tns.matmul(pn[b][:, :], st.ones_f[0:1, :],
                                  nrow[b][0:1, :], start=True, stop=True)
                inst.then_inc(a1bc, 1)
                tns.wait_ge(a1ap, t + 1)
                for o in range(OC):
                    gi = t * OC + o
                    if gi >= 2:
                        tns.wait_ge(a1ev, gi - 1)
                    for k in range(KC):
                        inst = tns.matmul(pmm[gi % 2][:, :], wsl(k, o), xnb[b][:, k, :],
                                          start=(k == 0), stop=(k == KC - 1))
                    inst.then_inc(a1mm, 1)

        @block.vector
        def _(v):
            v.wait_ge(st.ph, PH)
            for t in range(NTILES):
                b = t % 2
                v.wait_ge(a1x, 16 * (t + 1))
                v.wait_ge(a1bc, t + 1)
                if t >= 2:
                    v.wait_ge(s_xn, 16 * (t - 1))
                v.drain()
                for g in range(KC):
                    v.tensor_tensor(tf[:], xin[b][:, g, :], pr[b][:, :], ALU.mult)
                    v.drain()
                    inst = v.tensor_tensor(xnb[b][:, g, :], tf[:], pn[b][:, :], ALU.subtract)
                    v.drain()
                inst.then_inc(a1ap, 1)
                if t >= 1:
                    v.wait_ge(s_gt, 16 * t)
                for o in range(OC):
                    gi = t * OC + o
                    v.wait_ge(a1mm, gi + 1)
                    if o < GC:
                        v.tensor_scalar(xmp[b][:, o, 2:NT + 2], pmm[gi % 2][:, :],
                                        bcol[:, o:o + 1], 0.0, ALU.add, ALU.add).then_inc(a1ev, 1)
                    else:
                        og = o - GC
                        v.tensor_scalar(gt_sb[b][:, og, :], pmm[gi % 2][:, :],
                                        bcol[:, o:o + 1], 0.0, ALU.add, ALU.add).then_inc(a1ev, 1)
                v.drain()
                if t % 4 == 0:
                    v.memset(xmp[b][:, :, 0:2], 0.0)
                else:
                    v.tensor_copy(xmp[b][:, :, 0:2], xmp[1 - b][:, :, NT:NT + 2])
                v.drain()
                for g in range(GC):
                    gi = t * GC + g
                    if gi >= 2:
                        v.wait_ge(sem("a1si"), gi - 1)
                    v.tensor_scalar(c1[:], xmp[b][:, g, 0:NT], cw[:, 3 * g:3 * g + 1],
                                    0.0, ALU.mult, ALU.add)
                    v.drain()
                    v.scalar_tensor_tensor(c2[:], xmp[b][:, g, 1:NT + 1],
                                           cw[:, 3 * g + 1:3 * g + 2], c1[:],
                                           ALU.mult, ALU.add)
                    v.drain()
                    v.scalar_tensor_tensor(c3[:, gi % 2, :], xmp[b][:, g, 2:NT + 2],
                                           cw[:, 3 * g + 2:3 * g + 3], c2[:],
                                           ALU.mult, ALU.add).then_inc(a1cv, 1)
                    v.drain()

        @block.scalar
        def _(s):
            s.wait_ge(st.ph, PH)
            for t in range(NTILES):
                b = t % 2
                for g in range(GC):
                    gi = t * GC + g
                    s.wait_ge(a1cv, gi + 1)
                    s.activation(xmc_sb[b][:, g, :], c3[:, gi % 2, :], AF.Silu,
                                 bias=cb[:, g:g + 1], scale=1.0).then_inc(sem("a1si"), 1)
                s.drain()
                s.wait_ge(a1ap, t + 1)
                if t >= 1:
                    s.wait_ge(s_xn, 16 * t)
                s.dma_start(out=xn0_t[:, :, t * NT:(t + 1) * NT],
                            in_=xnb[b][:, :, :]).then_inc(s_xn, 16)
                if t >= 1:
                    s.wait_ge(s_xmc, 16 * t)
                s.dma_start(out=xmc_t[:, :, t * NT:(t + 1) * NT],
                            in_=xmc_sb[b][:, :, :]).then_inc(s_xmc, 16)
                s.wait_ge(a1ev, 32 * (t + 1))
                if t >= 1:
                    s.wait_ge(s_gt, 16 * t)
                s.dma_start(out=gt_t[:, :, t * NT:(t + 1) * NT],
                            in_=gt_sb[b][:, :, :]).then_inc(s_gt, 16)

        barrier(st, [(s_xn, 16 * NTILES), (s_xmc, 16 * NTILES), (s_gt, 16 * NTILES)])


def phase_a2(st):
    nc, block, E, sem = st.nc, st.block, st.E, st.sem
    PH = st.phase_idx
    a2xn = sem("a2xn"); a2x = sem("a2x"); a2w = sem("a2w"); a2mx = sem("a2mx")
    a2mm = sem("a2mm"); a2er = sem("a2er"); a2ek = sem("a2ek"); a2ev = sem("a2ev")
    a2sq = sem("a2sq"); a2st = sem("a2st")
    with contextlib.ExitStack() as ps:
        PE = ps.enter_context
        w_sb = PE(nc.sbuf_tensor("a2_w", [P, KC * 3 * DM], BF16))
        mucol = PE(nc.sbuf_tensor("a2_mu", [P, 4 * KC], F32))
        xnm = [PE(nc.sbuf_tensor(f"a2_xnm{i}", [P, KC, NT + 1], BF16)) for i in range(2)]
        xin = [PE(nc.sbuf_tensor("a2_x0", [P, KC, NT], F32))] * 2
        xr = PE(nc.sbuf_tensor("a2_xr", [P, KC, NT], BF16))
        xk = PE(nc.sbuf_tensor("a2_xk", [P, KC, NT], BF16))
        t1 = PE(nc.sbuf_tensor("a2_t1", [P, NT], F32))
        r_sb = PE(nc.sbuf_tensor("a2_r", [P, KC, NT], BF16))
        ka_sb = PE(nc.sbuf_tensor("a2_ka", [P, KC, NT], BF16))
        rl = PE(nc.sbuf_tensor("a2_rl", [P, 2, NT], F32))
        xp_sb = [PE(nc.sbuf_tensor("a2_xp0", [P, KC, NT], F32))] * 2
        u1 = PE(nc.sbuf_tensor("a2_u1", [P, NT], F32))
        p_r = [PE(nc.psum_tensor(f"a2_pr{i}", [P, NT], F32)) for i in range(2)]
        p_k = [PE(nc.psum_tensor(f"a2_pk{i}", [P, NT], F32)) for i in range(2)]
        p_v = [PE(nc.psum_tensor(f"a2_pv{i}", [P, NT], F32)) for i in range(2)]

        def wsl(w, k, o):
            i = (w * KC * KC + k * KC + o) * P
            return w_sb[:, i:i + P]

        x_t = st.x_fm.rearrange("(g q) t -> q g t", q=P)
        xn0_t = st.xn0_d.rearrange("(g q) t -> q g t", q=P)
        xp_t = st.xprime_d.rearrange("(g q) t -> q g t", q=P)

        @block.gpsimd
        def _(g):
            g.wait_ge(st.ph, PH)
            nw = 0
            for w in range(3):
                for k in range(KC):
                    if nw:
                        g.wait_ge(a2w, nw)
                    g.dma_start(out=w_sb[:, (w * KC * KC + k * KC) * P:(w * KC * KC + (k + 1) * KC) * P],
                                in_=st.w_rkv[k * P:(k + 1) * P, w * DM:(w + 1) * DM]).then_inc(a2w, 16)
                    nw += 16
            g.wait_ge(a2w, nw)
            g.dma_start(out=mucol[:, 0:2 * KC], in_=st.mu_rk[:, :]).then_inc(a2w, 16)
            nw += 16

        @block.vector
        def _(v):
            v.wait_ge(st.ph, PH)
            v.wait_ge(a2w, 16 * (3 * KC + 1))
            v.tensor_scalar(mucol[:, 2 * KC:4 * KC], mucol[:, 0:2 * KC], -1.0, 1.0,
                            ALU.mult, ALU.add).then_inc(sem("a2mu"), 1)
            for t in range(NTILES):
                b = t % 2
                v.wait_ge(a2xn, 16 * (t + 1))
                v.drain()
                if t % 4 == 0:
                    v.memset(xnm[b][:, :, 0:1], 0.0)
                else:
                    v.tensor_copy(xnm[b][:, :, 0:1], xnm[1 - b][:, :, NT:NT + 1])
                v.drain()
                for g in range(KC):
                    v.tensor_scalar(t1[:], xnm[b][:, g, 0:NT], mucol[:, 2 * KC + g:2 * KC + g + 1],
                                    0.0, ALU.mult, ALU.add)
                    v.drain()
                    v.scalar_tensor_tensor(xr[:, g, :], xnm[b][:, g, 1:NT + 1],
                                           mucol[:, g:g + 1], t1[:], ALU.mult, ALU.add)
                    v.drain()
                    v.tensor_scalar(t1[:], xnm[b][:, g, 0:NT], mucol[:, 3 * KC + g:3 * KC + g + 1],
                                    0.0, ALU.mult, ALU.add)
                    v.drain()
                    inst = v.scalar_tensor_tensor(xk[:, g, :], xnm[b][:, g, 1:NT + 1],
                                                  mucol[:, KC + g:KC + g + 1], t1[:], ALU.mult, ALU.add)
                    v.drain()
                inst.then_inc(a2mx, 1)
                for o in range(KC):
                    gk = t * KC + o
                    v.wait_ge(a2mm, t * 3 * KC + KC + o + 1)
                    if gk >= 2:
                        v.wait_ge(a2sq, gk - 1)
                    v.tensor_scalar(rl[:, gk % 2, :], p_k[o % 2][:, :], 0.0, 0.0,
                                    ALU.max, ALU.add).then_inc(a2ek, 1)
                    v.drain()
                v.wait_ge(a2er, KC * (t + 1))
                v.wait_ge(a2x, 16 * (t + 1))
                if t >= 1:
                    v.wait_ge(a2st, 16 * t)
                for o in range(KC):
                    v.wait_ge(a2mm, t * 3 * KC + 2 * KC + o + 1)
                    v.scalar_tensor_tensor(u1[:], r_sb[:, o, :], st.onecol[:, 0:1], p_v[o % 2][:, :],
                                           ALU.mult, ALU.mult)
                    v.drain()
                    v.tensor_tensor(xp_sb[b][:, o, :], u1[:], xin[b][:, o, :],
                                    ALU.add).then_inc(a2ev, 1)
                    v.drain()

        @block.scalar
        def _(s):
            s.wait_ge(st.ph, PH)
            for t in range(NTILES):
                if t >= 1:
                    s.wait_ge(a2ev, KC * t)
                    s.wait_ge(a2mm, 3 * KC * t)
                for o in range(KC):
                    gr = t * KC + o
                    s.wait_ge(a2mm, t * 3 * KC + o + 1)
                    s.activation(r_sb[:, o, :], p_r[o % 2][:, :], AF.Sigmoid).then_inc(a2er, 1)
                    s.drain()
                for o in range(KC):
                    gk = t * KC + o
                    s.wait_ge(a2ek, gk + 1)
                    s.activation(ka_sb[:, o, :], rl[:, gk % 2, :], AF.Square).then_inc(a2sq, 1)
                    s.drain()
                s.wait_ge(a2ev, KC * (t + 1))
                s.drain()
                if t >= 1:
                    s.wait_ge(a2st, 16 * t)
                s.dma_start(out=xp_t[:, :, t * NT:(t + 1) * NT],
                            in_=xp_sb[t % 2][:, :, :]).then_inc(a2st, 16)

        @block.sync
        def _(sync):
            sync.wait_ge(st.ph, PH)
            for t in range(NTILES):
                if t >= 1:
                    sync.wait_ge(a2xn, 16 * t)
                    sync.wait_ge(a2mx, t)
                sync.dma_start(out=xnm[t % 2][:, :, 1:NT + 1],
                               in_=xn0_t[:, :, t * NT:(t + 1) * NT]).then_inc(a2xn, 16)
                sync.wait_ge(a2x, 16 * t)
                if t >= 1:
                    sync.wait_ge(a2ev, KC * t)
                sync.dma_start(out=xin[t % 2][:, :, :],
                               in_=x_t[:, :, t * NT:(t + 1) * NT]).then_inc(a2x, 16)

        @block.tensor
        def _(tns):
            tns.wait_ge(st.ph, PH)
            tns.wait_ge(a2w, 16 * (3 * KC + 1))
            tns.wait_ge(sem("a2mu"), 1)
            for t in range(NTILES):
                tns.wait_ge(a2mx, t + 1)
                for o in range(KC):
                    gr = t * KC + o
                    if gr >= 2:
                        tns.wait_ge(a2er, gr - 1)
                    for k in range(KC):
                        inst = tns.matmul(p_r[o % 2][:, :], wsl(0, k, o), xr[:, k, :],
                                          start=(k == 0), stop=(k == KC - 1))
                    inst.then_inc(a2mm, 1)
                for o in range(KC):
                    gk = t * KC + o
                    if gk >= 2:
                        tns.wait_ge(a2ek, gk - 1)
                    for k in range(KC):
                        inst = tns.matmul(p_k[o % 2][:, :], wsl(1, k, o), xk[:, k, :],
                                          start=(k == 0), stop=(k == KC - 1))
                    inst.then_inc(a2mm, 1)
                tns.wait_ge(a2sq, KC * (t + 1))
                for o in range(KC):
                    gv = t * KC + o
                    if gv >= 2:
                        tns.wait_ge(a2ev, gv - 1)
                    for k in range(KC):
                        inst = tns.matmul(p_v[o % 2][:, :], wsl(2, k, o), ka_sb[:, k, :],
                                          start=(k == 0), stop=(k == KC - 1))
                    inst.then_inc(a2mm, 1)

        barrier(st, [(a2st, 16 * NTILES)])


def phase_b(st):
    with contextlib.ExitStack() as bs:
        st.y_all = bs.enter_context(st.nc.sbuf_tensor("y_all", [64, T], F32))
        phase_b1(st)
        phase_b2(st)


def phase_b1(st):
    nc, block, E, sem = st.nc, st.block, st.E, st.sem
    PH = st.phase_idx
    b1x = sem("b1x"); b1w = sem("b1w"); b1mm = sem("b1mm"); b1ac = sem("b1ac")
    b1sc = sem("b1sc"); b1st = sem("b1st"); b1g = sem("b1g"); b1f = sem("b1f")
    with contextlib.ExitStack() as ps:
        PE = ps.enter_context
        w_sb = PE(nc.sbuf_tensor("b1_w", [P, GC * 192], BF16))
        bcols = PE(nc.sbuf_tensor("b1_b", [64, 4], F32))
        xmc_sb = [PE(nc.sbuf_tensor(f"b1_xmc{i}", [P, GC, NT], BF16)) for i in range(2)]
        dt_sb = PE(nc.sbuf_tensor("b1_dt", [64, NT], F32))
        a_sb = PE(nc.sbuf_tensor("b1_a", [64, NT], F32))
        b_sb = PE(nc.sbuf_tensor("b1_bs", [64, NT], F32))
        ys_sb = PE(nc.sbuf_tensor("b1_ys", [64, NT], F32))
        carry = PE(nc.sbuf_tensor("b1_cy", [64, 1], F32))
        y_bf = PE(nc.sbuf_tensor("b1_ybf", [64, NT], BF16))
        y_sq = PE(nc.sbuf_tensor("b1_ysq", [64, NT], BF16))
        mrow = PE(nc.sbuf_tensor("b1_mrow", [1, T], F32))
        qrow = PE(nc.sbuf_tensor("b1_qrow", [1, T], F32))
        mg = PE(nc.sbuf_tensor("b1_mg", [P, T // P], F32))
        qg = PE(nc.sbuf_tensor("b1_qg", [P, T // P], F32))
        t1 = PE(nc.sbuf_tensor("b1_t1", [P, T // P], F32))
        rg = PE(nc.sbuf_tensor("b1_rg", [P, T // P], F32))
        ng = PE(nc.sbuf_tensor("b1_ng", [P, T // P], F32))
        p0 = [PE(nc.psum_tensor(f"b1_p0{i}", [P, NT], F32)) for i in range(2)]
        p1 = [PE(nc.psum_tensor(f"b1_p1{i}", [64, NT], F32)) for i in range(2)]
        psm = PE(nc.psum_tensor("b1_psm", [P, NT], F32))
        psq = PE(nc.psum_tensor("b1_psq", [P, NT], F32))

        xmc_t = st.xmc_d.rearrange("(g q) t -> q g t", q=P)

        @block.gpsimd
        def _(g):
            g.wait_ge(st.ph, PH)
            for k in range(GC):
                if k:
                    g.wait_ge(b1w, 16 * k)
                g.dma_start(out=w_sb[:, k * 192:(k + 1) * 192],
                            in_=st.w_dbc[k * P:(k + 1) * P, :]).then_inc(b1w, 16)
            g.wait_ge(b1w, 16 * GC)
            g.dma_start(out=bcols[:, 0:3], in_=st.b_dbc[:, :]).then_inc(b1w, 16)
            g.wait_ge(b1ac, 4 * NTILES)
            g.dma_start(out=st.stb_d[0:1, :], in_=mrow[0:1, :]).then_inc(b1g, 16)
            g.wait_ge(b1g, 16)
            g.dma_start(out=st.stb_d[1:2, :], in_=qrow[0:1, :]).then_inc(b1g, 16)
            g.wait_ge(b1g, 32)
            g.dma_start(out=mg[:, :],
                        in_=st.stb_d[0, :].rearrange("(p c) -> p c", p=P)).then_inc(b1g, 16)
            g.wait_ge(b1g, 48)
            g.dma_start(out=qg[:, :],
                        in_=st.stb_d[1, :].rearrange("(p c) -> p c", p=P)).then_inc(b1g, 16)
            g.wait_ge(b1f, 2)
            g.dma_start(out=st.stb_d[4, :].rearrange("(p c) -> p c", p=P),
                        in_=rg[:, :]).then_inc(b1g, 16)
            g.wait_ge(b1g, 80)
            g.dma_start(out=st.stb_d[5, :].rearrange("(p c) -> p c", p=P),
                        in_=ng[:, :]).then_inc(b1g, 16)

        @block.sync
        def _(sync):
            sync.wait_ge(st.ph, PH)
            for t in range(NTILES):
                if t >= 1:
                    sync.wait_ge(b1x, 16 * t)
                if t >= 2:
                    sync.wait_ge(b1mm, 3 * (t - 1))
                sync.dma_start(out=xmc_sb[t % 2][:, :, :],
                               in_=xmc_t[:, :, t * NT:(t + 1) * NT]).then_inc(b1x, 16)

        @block.tensor
        def _(tns):
            tns.wait_ge(st.ph, PH)
            tns.wait_ge(b1w, 16 * (GC + 1))
            for t in range(NTILES):
                b = t % 2
                tns.wait_ge(b1x, 16 * (t + 1))
                if t >= 2:
                    tns.wait_ge(b1ac, 4 * (t - 1))
                for k in range(GC):
                    tns.matmul(p0[b][:, :], w_sb[:, k * 192:k * 192 + 128],
                               xmc_sb[b][:, k, :], start=(k == 0), stop=(k == GC - 1))
                for k in range(GC):
                    inst = tns.matmul(p1[b][0:64, :], w_sb[:, k * 192 + 128:(k + 1) * 192],
                                      xmc_sb[b][:, k, :], start=(k == 0), stop=(k == GC - 1))
                inst.then_inc(b1mm, 1)
                tns.wait_ge(sem("b1cs"), 2 * (t + 1))
                if t >= 1:
                    tns.wait_ge(b1ac, 4 * t + 2)
                tns.matmul(psm[:, :], st.ones_bf[0:64, :], y_bf[:, :],
                           start=True, stop=True).then_inc(b1mm, 1)
                tns.matmul(psq[:, :], st.ones_bf[0:64, :], y_sq[:, :],
                           start=True, stop=True).then_inc(b1mm, 1)

        @block.scalar
        def _(s):
            s.wait_ge(st.ph, PH)
            s.wait_ge(b1w, 16 * (GC + 1))
            s.activation(bcols[:, 3:4], bcols[:, 0:1], AF.Copy, scale=-1.0).then_inc(b1sc, 1)
            for t in range(NTILES):
                b = t % 2
                s.wait_ge(b1mm, 3 * t + 1)
                s.drain()
                s.activation(dt_sb[:, :], p0[b][0:64, :], AF.Sigmoid,
                             bias=bcols[:, 0:1], scale=1.0)
                s.drain()
                s.activation(a_sb[:, :], p0[b][0:64, :], AF.Sigmoid,
                             bias=bcols[:, 3:4], scale=-1.0).then_inc(b1sc, 1)
                s.drain()
                s.wait_ge(b1ac, 4 * t + 3)
                if t >= 1:
                    s.wait_ge(b1mm, 3 * t)
                s.drain()
                s.activation(y_bf[:, :], st.y_all[:, t * NT:(t + 1) * NT],
                             AF.Copy).then_inc(sem("b1cs"), 1)
                s.drain()
                s.activation(y_sq[:, :], st.y_all[:, t * NT:(t + 1) * NT],
                             AF.Square).then_inc(sem("b1cs"), 1)
                s.drain()
            s.wait_ge(b1f, 1)
            s.drain()
            s.activation(t1[:], qg[:], AF.Ln, bias=st.epscol[:, 0:1], scale=1.0)
            s.drain()
            s.activation(rg[:], t1[:], AF.Exp, bias=0.0, scale=-0.5).then_inc(sem("b1rs"), 1)

        @block.vector
        def _(v):
            v.wait_ge(st.ph, PH)
            for t in range(NTILES):
                b = t % 2
                v.wait_ge(b1sc, t + 2)
                v.drain()
                v.scalar_tensor_tensor(b_sb[:, :], p0[b][64:128, :], bcols[:, 1:2],
                                       dt_sb[:, :], ALU.add, ALU.mult)
                v.drain()
                if t % 4 == 0:
                    v.tensor_tensor_scan(ys_sb[:, :], a_sb[:, :], b_sb[:, :], 0.0,
                                         ALU.mult, ALU.add)
                else:
                    v.tensor_tensor_scan(ys_sb[:, :], a_sb[:, :], b_sb[:, :],
                                         carry[:, 0:1], ALU.mult, ALU.add)
                v.drain()
                v.tensor_copy(carry[:, 0:1], ys_sb[:, NT - 1:NT])
                v.drain()
                v.scalar_tensor_tensor(st.y_all[:, t * NT:(t + 1) * NT], p1[b][0:64, :],
                                       bcols[:, 2:3], ys_sb[:, :],
                                       ALU.add, ALU.mult).then_inc(b1ac, 3)
                v.drain()
                v.wait_ge(b1mm, 3 * t + 3)
                v.tensor_scalar(mrow[0:1, t * NT:(t + 1) * NT], psm[0:1, :],
                                1.0 / 64, 0.0, ALU.mult, ALU.add)
                v.drain()
                v.tensor_scalar(qrow[0:1, t * NT:(t + 1) * NT], psq[0:1, :],
                                1.0 / 64, 0.0, ALU.mult, ALU.add).then_inc(b1ac, 1)
                v.drain()
            v.wait_ge(b1g, 64)
            v.tensor_tensor(t1[:], mg[:], mg[:], ALU.mult)
            v.drain()
            v.tensor_tensor(qg[:], qg[:], t1[:], ALU.subtract).then_inc(b1f, 1)
            v.drain()
            v.wait_ge(sem("b1rs"), 1)
            v.tensor_tensor(ng[:], mg[:], rg[:], ALU.mult).then_inc(b1f, 1)

        barrier(st, [(b1g, 96), (b1ac, 4 * NTILES)])


def phase_b2(st):
    nc, block, E, sem = st.nc, st.block, st.E, st.sem
    PH = st.phase_idx
    b2x = sem("b2x"); b2g = sem("b2g"); b2p = sem("b2p"); b2w = sem("b2w")
    b2r = sem("b2r"); b2bc = sem("b2bc"); b2mm = sem("b2mm"); b2yn = sem("b2yn")
    b2si = sem("b2si"); b2sg = sem("b2sg"); b2z = sem("b2z"); b2ev = sem("b2ev")
    b2ln = sem("b2ln"); b2st = sem("b2st"); b2gg = sem("b2gg"); b2f = sem("b2f")
    OC8 = KC
    with contextlib.ExitStack() as ps:
        PE = ps.enter_context
        wsi_sb = PE(nc.sbuf_tensor("b2_wsi", [64, INNER], BF16))
        wo_sb = PE(nc.sbuf_tensor("b2_wo", [P, GC * DM], BF16))
        bsi_row = PE(nc.sbuf_tensor("b2_bsi", [1, INNER], F32))
        bo_row = PE(nc.sbuf_tensor("b2_bo", [1, DM], F32))
        d0col = PE(nc.sbuf_tensor("b2_d0", [P, GC], F32))
        ones_row = PE(nc.sbuf_tensor("b2_ones", [1, NT], F32))
        rrow = [PE(nc.sbuf_tensor(f"b2_rr{i}", [1, NT], F32)) for i in range(2)]
        nrow = [PE(nc.sbuf_tensor(f"b2_nr{i}", [1, NT], F32)) for i in range(2)]
        xmc_sb = [PE(nc.sbuf_tensor("b2_xmc0", [P, GC, NT], BF16))] * 2
        gt_sb = [PE(nc.sbuf_tensor("b2_gt0", [P, GC, NT], BF16))] * 2
        xp_sb = [PE(nc.sbuf_tensor("b2_xp0", [P, KC, NT], F32))] * 2
        yn_bf = PE(nc.sbuf_tensor("b2_yn", [64, NT], BF16))
        tf = PE(nc.sbuf_tensor("b2_tf", [64, NT], F32))
        u1 = PE(nc.sbuf_tensor("b2_u1", [P, NT], F32))
        sg_sb = PE(nc.sbuf_tensor("b2_sg", [P, 2, NT], BF16))
        z_bf = PE(nc.sbuf_tensor("b2_z", [P, GC, NT], BF16))
        x1_sb = [PE(nc.sbuf_tensor("b2_x1", [P, KC, NT], F32))] * 2
        x1bf = PE(nc.sbuf_tensor("b2_x1bf", [P, 2, NT], BF16))
        x1sq = PE(nc.sbuf_tensor("b2_x1sq", [P, 2, NT], BF16))
        mrow = PE(nc.sbuf_tensor("b2_mrow", [1, T], F32))
        qrow = PE(nc.sbuf_tensor("b2_qrow", [1, T], F32))
        mg = PE(nc.sbuf_tensor("b2_mg", [P, T // P], F32))
        qg = PE(nc.sbuf_tensor("b2_qg", [P, T // P], F32))
        t1g = PE(nc.sbuf_tensor("b2_t1", [P, T // P], F32))
        rg = PE(nc.sbuf_tensor("b2_rg", [P, T // P], F32))
        ng = PE(nc.sbuf_tensor("b2_ng", [P, T // P], F32))
        pbc = PE(nc.psum_tensor("b2_pbc", [64, 2 * NT], F32))
        psi = [PE(nc.psum_tensor(f"b2_psi{i}", [P, NT], F32)) for i in range(2)]
        po = [PE(nc.psum_tensor(f"b2_po{i}", [P, NT], F32)) for i in range(2)]
        psm = PE(nc.psum_tensor("b2_psm", [P, NT], F32))
        psq = PE(nc.psum_tensor("b2_psq", [P, NT], F32))

        xmc_t = st.xmc_d.rearrange("(g q) t -> q g t", q=P)
        gt_t = st.gate0_d.rearrange("(g q) t -> q g t", q=P)
        xp_t = st.xprime_d.rearrange("(g q) t -> q g t", q=P)
        x1_t = st.x1_d.rearrange("(g q) t -> q g t", q=P)

        @block.gpsimd
        def _(g):
            g.wait_ge(st.ph, PH)
            nw = 0
            def w1(dst, srcq):
                nonlocal nw
                if nw:
                    g.wait_ge(b2w, nw)
                g.dma_start(out=dst, in_=srcq).then_inc(b2w, 16)
                nw += 16
            w1(wsi_sb[:, :], st.w_si[:, :])
            for k in range(GC):
                w1(wo_sb[:, k * DM:(k + 1) * DM], st.w_out0[k * P:(k + 1) * P, :])
            w1(bsi_row[:, :], st.b_si_row[:, :])
            w1(bo_row[:, :], st.b_out0_row[:, :])
            w1(d0col[:, :], st.d0[:, :])
            for t in range(NTILES):
                if t >= 1:
                    g.wait_ge(b2r, 32 * t)
                if t >= 2:
                    g.wait_ge(b2bc, t - 1)
                g.dma_start(out=rrow[t % 2][:, :],
                            in_=st.stb_d[4, t * NT:(t + 1) * NT].unsqueeze(0)).then_inc(b2r, 16)
                g.wait_ge(b2r, 32 * t + 16)
                g.dma_start(out=nrow[t % 2][:, :],
                            in_=st.stb_d[5, t * NT:(t + 1) * NT].unsqueeze(0)).then_inc(b2r, 16)
            g.wait_ge(b2ln, NTILES)
            g.dma_start(out=st.stb_d[0:1, :], in_=mrow[0:1, :]).then_inc(b2gg, 16)
            g.wait_ge(b2gg, 16)
            g.dma_start(out=st.stb_d[1:2, :], in_=qrow[0:1, :]).then_inc(b2gg, 16)
            g.wait_ge(b2gg, 32)
            g.dma_start(out=mg[:, :],
                        in_=st.stb_d[0, :].rearrange("(p c) -> p c", p=P)).then_inc(b2gg, 16)
            g.wait_ge(b2gg, 48)
            g.dma_start(out=qg[:, :],
                        in_=st.stb_d[1, :].rearrange("(p c) -> p c", p=P)).then_inc(b2gg, 16)
            g.wait_ge(b2f, 2)
            g.dma_start(out=st.stb_d[6, :].rearrange("(p c) -> p c", p=P),
                        in_=rg[:, :]).then_inc(b2gg, 16)
            g.wait_ge(b2gg, 80)
            g.dma_start(out=st.stb_d[7, :].rearrange("(p c) -> p c", p=P),
                        in_=ng[:, :]).then_inc(b2gg, 16)

        @block.sync
        def _(sync):
            sync.wait_ge(st.ph, PH)
            for t in range(NTILES):
                if t >= 1:
                    sync.wait_ge(b2x, 16 * t)
                if t >= 1:
                    sync.wait_ge(b2z, GC * t)
                sync.dma_start(out=xmc_sb[t % 2][:, :, :],
                               in_=xmc_t[:, :, t * NT:(t + 1) * NT]).then_inc(b2x, 16)
                if t >= 1:
                    sync.wait_ge(b2g, 16 * t)
                    sync.wait_ge(b2z, GC * t)
                sync.dma_start(out=gt_sb[t % 2][:, :, :],
                               in_=gt_t[:, :, t * NT:(t + 1) * NT]).then_inc(b2g, 16)
                if t >= 1:
                    sync.wait_ge(b2p, 16 * t)
                    sync.wait_ge(b2ev, KC * t)
                sync.dma_start(out=xp_sb[t % 2][:, :, :],
                               in_=xp_t[:, :, t * NT:(t + 1) * NT]).then_inc(b2p, 16)

        @block.tensor
        def _(tns):
            tns.wait_ge(st.ph, PH)
            tns.wait_ge(b2w, 16 * (GC + 4))
            tns.wait_ge(sem("b2os"), 1)
            for t in range(NTILES):
                b = t % 2
                tns.wait_ge(b2r, 32 * (t + 1))
                if t >= 1:
                    tns.wait_ge(b2yn, t)
                tns.matmul(pbc[0:64, 0:NT], st.ones_f[0:1, 0:64], rrow[b][0:1, :],
                           start=True, stop=True)
                tns.matmul(pbc[0:64, NT:2 * NT], st.ones_f[0:1, 0:64], nrow[b][0:1, :],
                           start=True, stop=True).then_inc(b2bc, 1)
                tns.wait_ge(b2yn, t + 1)
                for o in range(GC):
                    gsi = t * GC + o
                    if gsi >= 2:
                        tns.wait_ge(b2si, gsi - 1)
                    tns.matmul(psi[o % 2][:, :], bsi_row[0:1, o * P:(o + 1) * P],
                               ones_row[0:1, :], start=True, stop=False,
                               skip_group_check=True)
                    tns.matmul(psi[o % 2][:, :], wsi_sb[:, o * P:(o + 1) * P], yn_bf[:, :],
                               start=False, stop=True,
                               skip_group_check=True).then_inc(b2mm, 1)
                tns.wait_ge(b2z, GC * (t + 1))
                for o in range(OC8):
                    go = t * OC8 + o
                    if go >= 2:
                        tns.wait_ge(b2ev, go - 1)
                    tns.matmul(po[o % 2][:, :], bo_row[0:1, o * P:(o + 1) * P],
                               ones_row[0:1, :], start=True, stop=False,
                               skip_group_check=True)
                    for k in range(GC):
                        inst = tns.matmul(po[o % 2][:, :], wo_sb[:, (k * KC + o) * P:(k * KC + o + 1) * P],
                                          z_bf[:, k, :], start=False, stop=(k == GC - 1),
                                          skip_group_check=True)
                    inst.then_inc(b2mm, 1)
                if t >= 1:
                    tns.wait_ge(b2ln, t)
                for k in range(KC):
                    go = t * KC + k
                    tns.wait_ge(sem("b2cs"), 2 * (go + 1))
                    tns.matmul(psm[:, :], st.ones_bf[:, :], x1bf[:, k % 2, :],
                               start=(k == 0), stop=(k == KC - 1),
                               skip_group_check=True).then_inc(sem("b2sm"), 1)
                    tns.matmul(psq[:, :], st.ones_bf[:, :], x1sq[:, k % 2, :],
                               start=(k == 0), stop=(k == KC - 1),
                               skip_group_check=True).then_inc(sem("b2sm"), 1)

        @block.vector
        def _(v):
            v.wait_ge(st.ph, PH)
            v.memset(ones_row[:, :], 1.0).then_inc(sem("b2os"), 1)
            for t in range(NTILES):
                b = t % 2
                v.wait_ge(b2bc, t + 1)
                v.drain()
                v.tensor_tensor(tf[:, :], st.y_all[:, t * NT:(t + 1) * NT],
                                pbc[0:64, 0:NT], ALU.mult)
                v.drain()
                v.tensor_tensor(yn_bf[:, :], tf[:, :], pbc[0:64, NT:2 * NT],
                                ALU.subtract).then_inc(b2yn, 1)
                v.drain()
                v.wait_ge(b2x, 16 * (t + 1))
                for o in range(GC):
                    gsi = t * GC + o
                    v.wait_ge(b2mm, t * (GC + OC8) + o + 1)
                    v.wait_ge(b2sg, gsi + 1)
                    v.scalar_tensor_tensor(u1[:], xmc_sb[b][:, o, :], d0col[:, o:o + 1],
                                           psi[o % 2][:, :], ALU.mult, ALU.add).then_inc(b2si, 1)
                    v.drain()
                    v.tensor_tensor(z_bf[:, o, :], u1[:], sg_sb[:, gsi % 2, :],
                                    ALU.mult).then_inc(b2z, 1)
                    v.drain()
                v.wait_ge(b2p, 16 * (t + 1))
                if t >= 1:
                    v.wait_ge(b2st, 16 * t)
                for o in range(OC8):
                    go = t * OC8 + o
                    v.wait_ge(b2mm, t * (GC + OC8) + GC + o + 1)
                    v.tensor_tensor(x1_sb[b][:, o, :], po[o % 2][:, :], xp_sb[b][:, o, :],
                                    ALU.add).then_inc(b2ev, 1)
                    v.drain()
                v.wait_ge(sem("b2sm"), 2 * KC * (t + 1))
                v.tensor_scalar(mrow[0:1, t * NT:(t + 1) * NT], psm[0:1, :],
                                1.0 / DM, 0.0, ALU.mult, ALU.add)
                v.drain()
                v.tensor_scalar(qrow[0:1, t * NT:(t + 1) * NT], psq[0:1, :],
                                1.0 / DM, 0.0, ALU.mult, ALU.add).then_inc(b2ln, 1)
                v.drain()
            v.wait_ge(b2gg, 64)
            v.tensor_tensor(t1g[:], mg[:], mg[:], ALU.mult)
            v.drain()
            v.tensor_tensor(qg[:], qg[:], t1g[:], ALU.subtract).then_inc(b2f, 1)
            v.drain()
            v.wait_ge(sem("b2rs"), 1)
            v.tensor_tensor(ng[:], mg[:], rg[:], ALU.mult).then_inc(b2f, 1)

        @block.scalar
        def _(s):
            s.wait_ge(st.ph, PH)
            for t in range(NTILES):
                b = t % 2
                s.wait_ge(b2g, 16 * (t + 1))
                for o in range(GC):
                    gsi = t * GC + o
                    if gsi >= 2:
                        s.wait_ge(b2z, gsi - 1)
                    s.activation(sg_sb[:, gsi % 2, :], gt_sb[b][:, o, :],
                                 AF.Sigmoid).then_inc(b2sg, 1)
                    s.drain()
                for o in range(OC8):
                    go = t * OC8 + o
                    s.wait_ge(b2ev, go + 1)
                    if go >= 2:
                        s.wait_ge(sem("b2sm"), 2 * (go - 1))
                    s.drain()
                    s.activation(x1bf[:, go % 2, :], x1_sb[b][:, o, :], AF.Copy)
                    s.drain()
                    s.activation(x1sq[:, go % 2, :], x1_sb[b][:, o, :],
                                 AF.Square).then_inc(sem("b2cs"), 2)
                    s.drain()
                s.wait_ge(b2ev, KC * (t + 1))
                if t >= 1:
                    s.wait_ge(b2st, 16 * t)
                s.dma_start(out=x1_t[:, :, t * NT:(t + 1) * NT],
                            in_=x1_sb[b][:, :, :]).then_inc(b2st, 16)
            s.wait_ge(b2f, 1)
            s.drain()
            s.activation(t1g[:], qg[:], AF.Ln, bias=st.epscol[:, 0:1], scale=1.0)
            s.drain()
            s.activation(rg[:], t1g[:], AF.Exp, bias=0.0, scale=-0.5).then_inc(sem("b2rs"), 1)

        barrier(st, [(b2gg, 96), (b2st, 16 * NTILES)])


def col_to_cell(st, eng_v, g, col_ap, cell_ap, row_buf, mb_slot, dsem, vsem, base):
    g.dma_start(out=st.mb_d[mb_slot:mb_slot + 1, :],
                in_=col_ap.rearrange("p o -> o p")).then_inc(dsem, 16)
    g.wait_ge(dsem, base + 16)
    g.dma_start(out=row_buf, in_=st.mb_d[mb_slot:mb_slot + 1, :]).then_inc(dsem, 16)
    eng_v.wait_ge(dsem, base + 32)
    eng_v.tensor_reduce(cell_ap, row_buf, axis=AX.X, op=ALU.max).then_inc(vsem, 1)
    eng_v.drain()


def dequant_weight(st, ps, tag, w_nat, w_t, wq_sb, nblk, nchunks, oc_nat, after=None):
    nc, block, sem = st.nc, st.block, st.sem
    PE = ps.enter_context
    O = oc_nat * P
    dq = sem("dq_" + tag)
    dqv = sem("dqv_" + tag)
    dqt = sem("dqt_" + tag)
    dqs = sem("dqs_" + tag)
    nat = PE(nc.sbuf_tensor(f"dq_{tag}_nat", [P, nblk * 32], F32))
    s_all = PE(nc.sbuf_tensor(f"dq_{tag}_s", [P, oc_nat, nblk], F32))
    i_all = PE(nc.sbuf_tensor(f"dq_{tag}_i", [P, oc_nat, nblk], F32))
    sT = PE(nc.sbuf_tensor(f"dq_{tag}_sT", [nblk, O], F32))
    iT = PE(nc.sbuf_tensor(f"dq_{tag}_iT", [nblk, O], F32))
    wT = PE(nc.sbuf_tensor(f"dq_{tag}_wT", [P, O], F32))
    srep = PE(nc.sbuf_tensor(f"dq_{tag}_sr", [P, O], F32))
    irep = PE(nc.sbuf_tensor(f"dq_{tag}_ir", [P, O], F32))
    wf = PE(nc.sbuf_tensor(f"dq_{tag}_wf", [P, O], F32))
    ki = PE(nc.sbuf_tensor(f"dq_{tag}_ki", [P, O], F32))
    ptr = PE(nc.psum_tensor(f"dq_{tag}_pt", [nblk, P], F32))
    sT_dram = nc.dram_tensor(f"dq_{tag}_std", [nblk, O], F32).ap()
    iT_dram = nc.dram_tensor(f"dq_{tag}_itd", [nblk, O], F32).ap()
    rep_bn = 4 * (P // 128)

    @block.gpsimd
    def _(g):
        g.wait_ge(st.ph, st.phase_idx)
        if after is not None:
            g.wait_ge(after[0], after[1])
        n = 0
        for oc in range(oc_nat):
            if n:
                g.wait_ge(dq, n)
            if oc >= 1:
                g.wait_ge(dqv, oc)
            g.dma_start(out=nat[:, :], in_=w_nat[oc * P:(oc + 1) * P, :]).then_inc(dq, 16)
            n += 16
        g.wait_ge(dqs, 2 * oc_nat)
        g.wait_ge(dq, n)
        g.dma_start(out=sT_dram[:, :], in_=sT[:, :]).then_inc(dq, 16)
        n += 16
        g.wait_ge(dq, n)
        g.dma_start(out=iT_dram[:, :], in_=iT[:, :]).then_inc(dq, 16)
        n += 16
        for c in range(nchunks):
            g.wait_ge(dq, n)
            if c >= 1:
                g.wait_ge(dqv, oc_nat + c)
            g.dma_start(out=wT[:, :], in_=w_t[c * P:(c + 1) * P, :]).then_inc(dq, 16)
            n += 16
            g.wait_ge(dq, n)
            g.dma_start(out=srep[:, :],
                        in_=sT_dram[4 * c:4 * c + 4, :].unsqueeze(1).broadcast_to(
                            [4, 32, O])).then_inc(dq, 16)
            n += 16
            g.wait_ge(dq, n)
            g.dma_start(out=irep[:, :],
                        in_=iT_dram[4 * c:4 * c + 4, :].unsqueeze(1).broadcast_to(
                            [4, 32, O])).then_inc(dq, 16)
            n += 16

    @block.vector
    def _(v):
        v.wait_ge(st.ph, st.phase_idx)
        for oc in range(oc_nat):
            v.wait_ge(dq, 16 * (oc + 1))
            v.drain()
            v.tensor_reduce(s_all[:, oc, :],
                            nat[:, :].rearrange("p (b i) -> p b i", b=nblk),
                            axis=AX.X, op=ALU.max, apply_absolute_value=True)
            v.drain()
            v.tensor_scalar(s_all[:, oc, :], s_all[:, oc, :], st.c7col[:, 1:2],
                            1e-12, ALU.mult, ALU.max)
            v.drain()
            v.reciprocal(i_all[:, oc, :], s_all[:, oc, :]).then_inc(dqv, 1)
            v.drain()
        for c in range(nchunks):
            v.wait_ge(dq, 16 * (oc_nat + 2) + 48 * (c + 1))
            v.drain()
            v.tensor_tensor(wf[:, :], wT[:, :], irep[:, :], ALU.mult)
            v.drain()
            v.tensor_scalar(wf[:, :], wf[:, :], MAGIC, MAGIC, ALU.add, ALU.subtract)
            v.drain()
            v.tensor_scalar(ki[:, :], wf[:, :], 7.0, -8.0, ALU.min, ALU.max)
            v.drain()
            v.tensor_tensor(wq_sb[:, c * O:(c + 1) * O], ki[:, :], srep[:, :],
                            ALU.mult).then_inc(dqv, 1)
            v.drain()

    @block.tensor
    def _(tns):
        tns.wait_ge(st.ph, st.phase_idx)
        k = 0
        for oc in range(oc_nat):
            tns.wait_ge(dqv, oc + 1)
            for which in range(2):
                k += 1
                if k >= 2:
                    tns.wait_ge(dqs, k - 1)
                src_ap = (s_all if which == 0 else i_all)[:, oc, :]
                tns.transpose(ptr[:, :], src_ap, st.ident_f[:, :]).then_inc(dqt, 1)

    @block.scalar
    def _(s):
        s.wait_ge(st.ph, st.phase_idx)
        k = 0
        for oc in range(oc_nat):
            for which in range(2):
                k += 1
                s.wait_ge(dqt, k)
                dst = (sT if which == 0 else iT)[:, oc * P:(oc + 1) * P]
                s.drain()
                s.activation(dst, ptr[:, :], AF.Copy).then_inc(dqs, 1)
                s.drain()

    return dqv


def phase_d(st):
    phase_d1(st)
    phase_d2(st)


def phase_d1(st):
    nc, block, E, sem = st.nc, st.block, st.E, st.sem
    PH = st.phase_idx
    dx = sem("d1x"); dw = sem("d1w"); dr = sem("d1r"); dbc = sem("d1bc")
    dap = sem("d1ap"); dmm = sem("d1mm"); dev = sem("d1ev"); dst_ = sem("d1st")
    dsx = sem("d1sx"); dcc = sem("d1cc"); ddb = sem("d1db")
    OCW = 10
    with contextlib.ExitStack() as ps:
        PE = ps.enter_context
        w_sb = PE(nc.sbuf_tensor("d1_w", [P, KC * 1184], BF16))
        rrow = [PE(nc.sbuf_tensor(f"d1_rr{i}", [1, NT], F32)) for i in range(2)]
        nrow = [PE(nc.sbuf_tensor(f"d1_nr{i}", [1, NT], F32)) for i in range(2)]
        xin = [PE(nc.sbuf_tensor("d1_x0", [P, KC, NT], F32))] * 2
        xnb = [PE(nc.sbuf_tensor(f"d1_xn{i}", [P, KC, NT], BF16)) for i in range(2)]
        tf = PE(nc.sbuf_tensor("d1_tf", [P, NT], F32))
        xb_sb = [PE(nc.sbuf_tensor("d1_xb0", [P, OCW, NT], BF16))] * 2
        dt32 = [PE(nc.sbuf_tensor(f"d1_dt{i}", [32, NT], F32)) for i in range(2)]
        mxc = PE(nc.sbuf_tensor("d1_mx", [P, 1], F32))
        mtmp = PE(nc.sbuf_tensor("d1_mt", [P, 1], F32))
        mrow_b = PE(nc.sbuf_tensor("d1_mrow", [1, P], F32))
        cells = PE(nc.sbuf_tensor("d1_cells", [1, 16], F32))
        ccs = PE(nc.sbuf_tensor("d1_ccs", [1, 16], F32))
        pr = [PE(nc.psum_tensor(f"d1_pr{i}", [P, NT], F32)) for i in range(2)]
        pn = [PE(nc.psum_tensor(f"d1_pn{i}", [P, NT], F32)) for i in range(2)]
        pmm = [PE(nc.psum_tensor(f"d1_p{i}", [P, NT], F32)) for i in range(2)]

        x1_t = st.x1_d.rearrange("(g q) t -> q g t", q=P)
        xn1_t = st.xn1_d.rearrange("(g q) t -> q g t", q=P)

        @block.gpsimd
        def _(g):
            g.wait_ge(st.ph, PH)
            nw = 0
            for k in range(KC):
                if nw:
                    g.wait_ge(dw, nw)
                g.dma_start(out=w_sb[:, k * 1184:(k + 1) * 1184],
                            in_=st.w_in1[k * P:(k + 1) * P, :]).then_inc(dw, 16)
                nw += 16
            for t in range(NTILES):
                if t >= 1:
                    g.wait_ge(dr, 32 * t)
                if t >= 2:
                    g.wait_ge(dbc, t - 1)
                g.dma_start(out=rrow[t % 2][:, :],
                            in_=st.stb_d[6, t * NT:(t + 1) * NT].unsqueeze(0)).then_inc(dr, 16)
                g.wait_ge(dr, 32 * t + 16)
                g.dma_start(out=nrow[t % 2][:, :],
                            in_=st.stb_d[7, t * NT:(t + 1) * NT].unsqueeze(0)).then_inc(dr, 16)
            g.wait_ge(dap, NTILES)
            g.dma_start(out=st.mb_d[0:1, :], in_=mxc[:, 0:1]).then_inc(ddb, 16)
            g.wait_ge(ddb, 16)
            g.dma_start(out=mrow_b[0:1, :], in_=st.mb_d[0:1, :]).then_inc(ddb, 16)
            g.wait_ge(dsx, 2)
            g.dma_start(out=st.cc_in_d[:, :], in_=ccs[0:1, :]).then_inc(ddb, 16)
            g.wait_ge(ddb, 48)
            g.collective_compute(
                "AllReduce", ALU.max,
                replica_groups=[[0, 1, 2, 3, 4, 5, 6, 7]],
                ins=[st.cc_in_d[:, :]], outs=[st.cc_out_ds[0][:, :]],
            ).then_inc(dcc, 1)
            g.wait_ge(dcc, 1)
            g.dma_start(out=cells[0:1, :], in_=st.cc_out_ds[0][:, :]).then_inc(ddb, 16)
            g.wait_ge(dsx, 3)
            g.dma_start(out=st.scl_d[0:1, 0:16], in_=ccs[0:1, :]).then_inc(ddb, 16)

        @block.sync
        def _(sync):
            sync.wait_ge(st.ph, PH)
            for t in range(NTILES):
                if t >= 1:
                    sync.wait_ge(dx, 16 * t)
                    sync.wait_ge(dap, t)
                sync.dma_start(out=xin[t % 2][:, :, :],
                               in_=x1_t[:, :, t * NT:(t + 1) * NT]).then_inc(dx, 16)

        @block.tensor
        def _(tns):
            tns.wait_ge(st.ph, PH)
            tns.wait_ge(dw, 16 * KC)
            for t in range(NTILES):
                b = t % 2
                tns.wait_ge(dr, 32 * (t + 1))
                if t >= 2:
                    tns.wait_ge(dap, t - 1)
                tns.matmul(pr[b][:, :], st.ones_f[0:1, :], rrow[b][0:1, :],
                           start=True, stop=True)
                tns.matmul(pn[b][:, :], st.ones_f[0:1, :], nrow[b][0:1, :],
                           start=True, stop=True).then_inc(dbc, 1)
                tns.wait_ge(dap, t + 1)
                for o in range(OCW):
                    go = t * OCW + o
                    w0 = o * P
                    mw = P if o < 9 else 32
                    if go >= 2:
                        tns.wait_ge(dev, go - 1)
                    for k in range(KC):
                        inst = tns.matmul(pmm[go % 2][0:mw, :],
                                          w_sb[:, k * 1184 + w0:k * 1184 + w0 + mw],
                                          xnb[b][:, k, :],
                                          start=(k == 0), stop=(k == KC - 1))
                    inst.then_inc(dmm, 1)

        @block.vector
        def _(v):
            v.wait_ge(st.ph, PH)
            v.memset(mxc[:, :], 0.0)
            for t in range(NTILES):
                b = t % 2
                v.wait_ge(dx, 16 * (t + 1))
                v.wait_ge(dbc, t + 1)
                if t >= 1:
                    v.wait_ge(dst_, 48 * t)
                v.drain()
                for g in range(KC):
                    v.tensor_tensor(tf[:], xin[b][:, g, :], pr[b][:, :], ALU.mult)
                    v.drain()
                    v.tensor_tensor(xnb[b][:, g, :], tf[:], pn[b][:, :], ALU.subtract)
                    v.drain()
                    v.tensor_reduce(mtmp[:, 0:1], xnb[b][:, g, :], axis=AX.X,
                                    op=ALU.max, apply_absolute_value=True)
                    v.drain()
                    inst = v.tensor_tensor(mxc[:, 0:1], mxc[:, 0:1], mtmp[:, 0:1], ALU.max)
                    v.drain()
                inst.then_inc(dap, 1)
                for o in range(OCW):
                    go = t * OCW + o
                    mw = P if o < 9 else 32
                    v.wait_ge(dmm, go + 1)
                    if o == 9:
                        v.tensor_copy(dt32[b][:, :], pmm[go % 2][0:32, :])
                        v.drain()
                    v.tensor_copy(xb_sb[b][0:mw, o, :], pmm[go % 2][0:mw, :]).then_inc(dev, 1)
                    v.drain()
            v.wait_ge(ddb, 32)
            v.tensor_reduce(cells[0:1, 0:1], mrow_b[0:1, :], axis=AX.X, op=ALU.max)
            v.drain()
            v.memset(ccs[0:1, :], 0.0)
            v.drain()
            v.tensor_copy(ccs[0:1, 0:1], cells[0:1, 0:1]).then_inc(dsx, 2)
            v.drain()
            v.wait_ge(ddb, 64)
            v.tensor_tensor(ccs[0:1, 0:1], cells[0:1, 0:1], st.c327[0:1, 1:2], ALU.mult)
            v.drain()
            v.tensor_scalar(ccs[0:1, 0:1], ccs[0:1, 0:1], 1e-12, 0.0, ALU.max, ALU.add)
            v.drain()
            v.reciprocal(ccs[0:1, 1:2], ccs[0:1, 0:1])
            v.drain()
            v.nop().then_inc(dsx, 1)

        @block.scalar
        def _(s):
            s.wait_ge(st.ph, PH)
            for t in range(NTILES):
                b = t % 2
                s.wait_ge(dap, t + 1)
                if t >= 1:
                    s.wait_ge(dst_, 48 * t)
                s.drain()
                s.dma_start(out=xn1_t[:, :, t * NT:(t + 1) * NT],
                            in_=xnb[b][:, :, :]).then_inc(dst_, 16)
                s.wait_ge(dev, OCW * (t + 1))
                s.wait_ge(dst_, 48 * t + 16)
                s.drain()
                s.dma_start(out=st.xbcdt_d[0:1152, t * NT:(t + 1) * NT].rearrange(
                                "(g q) t -> q g t", q=P),
                            in_=xb_sb[b][:, 0:9, :]).then_inc(dst_, 16)
                s.wait_ge(dst_, 48 * t + 32)
                s.dma_start(out=st.dtpre_d[:, t * NT:(t + 1) * NT],
                            in_=dt32[b][:, :]).then_inc(dst_, 16)

        barrier(st, [(dst_, 48 * NTILES), (ddb, 80)])


def phase_d2(st):
    nc, block, E, sem = st.nc, st.block, st.E, st.sem
    PH = st.phase_idx
    ex = sem("d2x"); emm = sem("d2mm"); eev = sem("d2ev"); est = sem("d2st")
    exq = sem("d2xq"); ecc = sem("d2cc")
    edb = sem("d2db"); efin = sem("d2fin")
    ecl = sem("d2cl"); esl = sem("d2sl")
    OC24 = 24
    with contextlib.ExitStack() as ps:
        PE = ps.enter_context
        wq_up = PE(nc.sbuf_tensor("d2_wu", [P, KC * 3072], BF16))
        wq_g = PE(nc.sbuf_tensor("d2_wg", [P, KC * 3072], BF16))
        with contextlib.ExitStack() as dqs_scope:
            dequant_weight(st, dqs_scope, "up", st.w_up_n, st.w_up_t, wq_up, 32, KC, OC24)
        with contextlib.ExitStack() as dqs_scope:
            dequant_weight(st, dqs_scope, "ga", st.w_gate_n, st.w_gate_t, wq_g, 32, KC, OC24,
                           after=(sem("dqv_up"), OC24 + KC))
        xnb = [PE(nc.sbuf_tensor(f"d2_xn{i}", [P, KC, NT], BF16)) for i in range(2)]
        kint = PE(nc.sbuf_tensor("d2_ki", [P, KC, NT], F32))
        xq = PE(nc.sbuf_tensor("d2_xq", [P, KC, NT], BF16))
        up_sb = PE(nc.sbuf_tensor("d2_up", [P, OC24, NT], BF16))
        gt_sb = PE(nc.sbuf_tensor("d2_gt", [P, OC24, NT], BF16))
        scol = PE(nc.sbuf_tensor("d2_scol", [P, 2], F32))
        mxc = PE(nc.sbuf_tensor("d2_mx", [P, 4], F32))
        mrow_b = PE(nc.sbuf_tensor("d2_mrow", [1, P], F32))
        cells = PE(nc.sbuf_tensor("d2_cells", [1, 16], F32))
        ccs = PE(nc.sbuf_tensor("d2_ccs", [1, 16], F32))
        pmm = [PE(nc.psum_tensor(f"d2_p{i}", [P, NT], F32)) for i in range(2)]

        xn1_t = st.xn1_d.rearrange("(g q) t -> q g t", q=P)
        up_t = st.up_d.rearrange("(g q) t -> q g t", q=P)
        gt_t = st.gate1_d.rearrange("(g q) t -> q g t", q=P)

        @block.sync
        def _(sync):
            sync.wait_ge(st.ph, PH)
            sync.wait_ge(sem("dqv_ga"), OC24 + KC)
            sync.dma_start(out=scol[:, 0:2],
                           in_=st.scl_d[0:1, 0:2].broadcast_to([P, 2])).then_inc(esl, 16)
            for t in range(NTILES):
                if t >= 1:
                    sync.wait_ge(ex, 16 * t)
                if t >= 2:
                    sync.wait_ge(exq, t - 1)
                sync.dma_start(out=xnb[t % 2][:, :, :],
                               in_=xn1_t[:, :, t * NT:(t + 1) * NT]).then_inc(ex, 16)

        @block.vector
        def _(v):
            v.wait_ge(st.ph, PH)
            v.wait_ge(esl, 16)
            v.memset(mxc[:, :], 0.0)
            for t in range(NTILES):
                b = t % 2
                v.wait_ge(ex, 16 * (t + 1))
                if t >= 1:
                    v.wait_ge(emm, 2 * OC24 * t)
                v.drain()
                v.tensor_scalar(kint[:, :, :], xnb[b][:, :, :], scol[:, 1:2], MAGIC,
                                ALU.mult, ALU.add)
                v.drain()
                v.tensor_scalar(xq[:, :, :], kint[:, :, :], MAGIC, scol[:, 0:1],
                                ALU.subtract, ALU.mult).then_inc(exq, 1)
                v.drain()
                for w in range(2):
                    stage = up_sb if w == 0 else gt_sb
                    for o in range(OC24):
                        go = 2 * OC24 * t + w * OC24 + o
                        v.wait_ge(emm, go + 1)
                        if t >= 1 and o == 0 and w == 0:
                            v.wait_ge(est, 32 * t)
                        v.tensor_copy(stage[:, o, :], pmm[go % 2][:, :]).then_inc(eev, 1)
                        v.drain()
                for w in range(2):
                    stage = up_sb if w == 0 else gt_sb
                    v.tensor_reduce(mxc[:, 2 + w:3 + w],
                                    stage[:, :, :].rearrange("p a b -> p (a b)"),
                                    axis=AX.X, op=ALU.max, apply_absolute_value=True)
                    v.drain()
                    v.tensor_tensor(mxc[:, w:w + 1], mxc[:, w:w + 1],
                                    mxc[:, 2 + w:3 + w], ALU.max)
                    v.drain()
            v.nop().then_inc(efin, 1)
            v.wait_ge(edb, 32)
            v.tensor_reduce(cells[0:1, 1:2], mrow_b[0:1, :], axis=AX.X,
                            op=ALU.max).then_inc(ecl, 1)
            v.drain()
            v.wait_ge(edb, 64)
            v.tensor_reduce(cells[0:1, 2:3], mrow_b[0:1, :], axis=AX.X,
                            op=ALU.max).then_inc(ecl, 1)
            v.drain()
            v.memset(ccs[0:1, :], 0.0)
            v.drain()
            v.tensor_copy(ccs[0:1, 1:3], cells[0:1, 1:3]).then_inc(efin, 1)
            v.drain()
            v.wait_ge(edb, 96)
            v.tensor_tensor(ccs[0:1, 2:4], cells[0:1, 1:3],
                            st.c327[0:1, 1:2].broadcast_to([1, 2]), ALU.mult)
            v.drain()
            v.tensor_scalar(ccs[0:1, 2:4], ccs[0:1, 2:4], 1e-12, 0.0, ALU.max, ALU.add)
            v.drain()
            v.reciprocal(ccs[0:1, 4:6], ccs[0:1, 2:4]).then_inc(efin, 1)
            v.drain()

        @block.tensor
        def _(tns):
            tns.wait_ge(st.ph, PH)
            tns.wait_ge(sem("dqv_up"), OC24 + KC)
            tns.wait_ge(sem("dqv_ga"), OC24 + KC)
            for t in range(NTILES):
                tns.wait_ge(exq, t + 1)
                for w in range(2):
                    wq = wq_up if w == 0 else wq_g
                    for o in range(OC24):
                        go = 2 * OC24 * t + w * OC24 + o
                        if go >= 2:
                            tns.wait_ge(eev, go - 1)
                        for k in range(KC):
                            inst = tns.matmul(pmm[go % 2][:, :],
                                              wq[:, k * 3072 + o * P:k * 3072 + (o + 1) * P],
                                              xq[:, k, :],
                                              start=(k == 0), stop=(k == KC - 1))
                        inst.then_inc(emm, 1)

        @block.scalar
        def _(s):
            s.wait_ge(st.ph, PH)
            for t in range(NTILES):
                s.wait_ge(eev, 2 * OC24 * t + OC24)
                if t >= 1:
                    s.wait_ge(est, 32 * t)
                s.drain()
                s.dma_start(out=up_t[:, :, t * NT:(t + 1) * NT],
                            in_=up_sb[:, :, :]).then_inc(est, 16)
                s.wait_ge(eev, 2 * OC24 * (t + 1))
                s.wait_ge(est, 32 * t + 16)
                s.drain()
                s.dma_start(out=gt_t[:, :, t * NT:(t + 1) * NT],
                            in_=gt_sb[:, :, :]).then_inc(est, 16)

        @block.gpsimd
        def _(g):
            g.wait_ge(st.ph, PH)
            g.wait_ge(efin, 1)
            g.dma_start(out=st.mb_d[1:2, :], in_=mxc[:, 0:1]).then_inc(edb, 16)
            g.wait_ge(edb, 16)
            g.dma_start(out=mrow_b[0:1, :], in_=st.mb_d[1:2, :]).then_inc(edb, 16)
            g.wait_ge(ecl, 1)
            g.dma_start(out=st.mb_d[2:3, :], in_=mxc[:, 1:2]).then_inc(edb, 16)
            g.wait_ge(edb, 48)
            g.dma_start(out=mrow_b[0:1, :], in_=st.mb_d[2:3, :]).then_inc(edb, 16)
            g.wait_ge(efin, 2)
            g.dma_start(out=st.cc_in_d[:, :], in_=ccs[0:1, :]).then_inc(edb, 16)
            g.wait_ge(edb, 80)
            g.collective_compute(
                "AllReduce", ALU.max,
                replica_groups=[[0, 1, 2, 3, 4, 5, 6, 7]],
                ins=[st.cc_in_d[:, :]], outs=[st.cc_out_ds[1][:, :]],
            ).then_inc(ecc, 1)
            g.wait_ge(ecc, 1)
            g.dma_start(out=cells[0:1, :], in_=st.cc_out_ds[1][:, :]).then_inc(edb, 16)
            g.wait_ge(efin, 3)
            g.dma_start(out=st.scl_d[0:1, 16:20], in_=ccs[0:1, 2:6]).then_inc(edb, 16)

        barrier(st, [(edb, 112), (est, 32 * NTILES)])


def phase_e(st):
    nc, block, E, sem = st.nc, st.block, st.E, st.sem
    PH = st.phase_idx
    CS = 64
    NCH = SEQ // CS
    NCR = st.e_nch
    WIN = 8
    NC2 = 2 * NCH
    ex = sem("e_x"); ewo = sem("e_wo"); esp = sem("e_sp"); epv = sem("e_pv")
    etx = sem("e_tx"); etr = sem("e_tr"); eta = sem("e_ta"); eab = sem("e_ab")
    esg = sem("e_sg"); exp_ = sem("e_xp"); ecb = sem("e_cb"); ecv = sem("e_cv")
    emh = sem("e_mh"); es4 = sem("e_s4"); ept = sem("e_pt"); edm = sem("e_dm")
    eav = sem("e_av"); eds = sem("e_ds"); ede = sem("e_de"); exd = sem("e_xd")
    et1 = sem("e_t1"); eym = sem("e_ym"); eya = sem("e_ya"); eyv = sem("e_yv")
    esm = sem("e_sm"); esa = sem("e_sa"); ebt = sem("e_bt"); ewm = sem("e_wm")
    ewd = sem("e_wd"); ewv = sem("e_wv"); egd = sem("est_cp")
    with contextlib.ExitStack() as ps:
        PE = ps.enter_context
        wo_sb = PE(nc.sbuf_tensor("e_wo", [P, KC * DM], BF16))
        bcols = PE(nc.sbuf_tensor("e_bc", [32, 4], F32))
        dcol = PE(nc.sbuf_tensor("e_dcol", [P, KC], F32))
        dtp = PE(nc.sbuf_tensor("e_dtp", [32, SEQ], F32))
        an_t = dtp
        zf = PE(nc.sbuf_tensor("e_zf", [32, SEQ], F32))
        t32 = PE(nc.sbuf_tensor("e_t32", [32, SEQ], F32))
        dt_f = PE(nc.sbuf_tensor("e_dt", [32, SEQ], F32))
        ac = PE(nc.sbuf_tensor("e_ac", [32, SEQ], F32))
        z64 = PE(nc.sbuf_tensor("e_z64", [32, CS], F32))
        acT = PE(nc.sbuf_tensor("e_acT", [CS, NCH * 32], F32))
        dtT = PE(nc.sbuf_tensor("e_dtT", [CS, NCH * 32], BF16))
        acfih = PE(nc.sbuf_tensor("e_afi", [127, SEQ], F32))
        anb = [PE(nc.sbuf_tensor(f"e_anb{i}", [127, P], F32)) for i in range(2)]
        xsd = [PE(nc.sbuf_tensor(f"e_xsd{i}", [P, KC, CS], BF16)) for i in range(2)]
        b_sb = [PE(nc.sbuf_tensor(f"e_bch{i}", [CS, CS], BF16)) for i in range(2)]
        c_sb = [PE(nc.sbuf_tensor(f"e_cch{i}", [CS, CS], BF16)) for i in range(2)]
        cbc = [PE(nc.sbuf_tensor(f"e_cbc{i}", [P, CS], BF16)) for i in range(2)]
        btT = [PE(nc.sbuf_tensor(f"e_btT{i}", [CS, CS], BF16)) for i in range(2)]
        lm = PE(nc.sbuf_tensor("e_lm", [P, SEQ], BF16))
        msb = PE(nc.sbuf_tensor("e_msb", [P, SEQ], BF16))
        stg4 = PE(nc.sbuf_tensor("e_stg", [P, 8192], BF16))
        xdec = PE(nc.sbuf_tensor("e_xdec", [CS, 1024], BF16))
        decT = PE(nc.sbuf_tensor("e_decT", [CS, 32], F32))
        cdb32 = PE(nc.sbuf_tensor("e_cdb", [CS, 32], F32))
        acE = PE(nc.sbuf_tensor("e_acE", [1, 32], F32))
        s_sb = PE(nc.sbuf_tensor("e_s", [CS, 1024], F32))
        s_t1 = PE(nc.sbuf_tensor("e_st1", [CS, 1024], F32))
        yv = PE(nc.sbuf_tensor("e_yv", [P, KC, CS], BF16))
        yw = PE(nc.sbuf_tensor("e_yw", [P, KC, NT], BF16))
        wout = PE(nc.sbuf_tensor("e_wout", [P, 2, NT], F32))
        ps_seg = [PE(nc.psum_tensor(f"e_pseg{i}", [P, 1024], F32)) for i in range(2)]
        ps_y = [PE(nc.psum_tensor(f"e_py{i}", [P, NT], F32)) for i in range(2)]
        ps_tb = PE(nc.psum_tensor("e_ptb", [CS, 576], BF16))
        ps_sm = PE(nc.psum_tensor("e_psm", [CS, 384], F32))
        ptr_d = lambda: ps_sm[0:CS, 128:256]
        ptr_a = lambda: ps_sm[0:CS, 0:128]
        ps_cb = lambda: ps_sm[0:CS, 256:320]
        ps_dc = lambda: ps_sm[0:CS, 320:352]
        ps_ae = lambda: ps_sm[0:1, 352:384]
        ps_st = [lambda: ps_seg[1][0:CS, 0:512], lambda: ps_seg[1][0:CS, 512:1024]]
        ps_w = [lambda: ps_seg[0][:, 0:512], lambda: ps_seg[0][:, 512:1024]]

        xbx_t = st.xbcdt_d[0:1024, :].rearrange("(g q) t -> q g t", q=P)

        def stg_bands(r0, r1, h0=0, h1=32):
            return stg4[r0:r1, :].rearrange(
                "j (h x) -> j h x", h=32)[:, h0:h1, 0:32]

        @block.sync
        def _(sync):
            sync.wait_ge(st.ph, PH)
            nw = 0
            for (dst, src) in [(bcols[:, 0:1], st.dtb1[:, :]),
                               (bcols[:, 1:2], st.alog1[:, :])]:
                if nw:
                    sync.wait_ge(ewo, nw)
                sync.dma_start(out=dst, in_=src).then_inc(ewo, 16)
                nw += 16
            for g in range(KC):
                sync.wait_ge(ewo, nw)
                sync.dma_start(out=dcol[:, g:g + 1],
                               in_=st.d1[4 * g:4 * g + 4, 0:1].unsqueeze(1)
                               .broadcast_to([4, 32, 1])).then_inc(ewo, 16)
                nw += 16
            for cc in range(2 * NCR):
                s, c = cc // NCR, cc % NCR
                pb = cc % 2
                tok0 = s * SEQ + c * CS
                if cc >= 2:
                    sync.wait_ge(eyv, cc - 1)
                    sync.wait_ge(etx, 9 * (cc - 1))
                    sync.wait_ge(ecb, cc - 1)
                    sync.wait_ge(emh, 2 * (cc - 1))
                if cc >= 1:
                    sync.wait_ge(ex, 64 * cc)
                sync.dma_start(out=xsd[pb][:, :, :],
                               in_=xbx_t[:, :, tok0:tok0 + CS]).then_inc(ex, 16)
                sync.wait_ge(ex, 64 * cc + 16)
                sync.dma_start(out=b_sb[pb][:, :],
                               in_=st.xbcdt_d[1024:1088, tok0:tok0 + CS]).then_inc(ex, 16)
                sync.wait_ge(ex, 64 * cc + 32)
                sync.dma_start(out=c_sb[pb][:, :],
                               in_=st.xbcdt_d[1088:1152, tok0:tok0 + CS]).then_inc(ex, 16)
                sync.wait_ge(ex, 64 * cc + 48)
                sync.dma_start(out=cbc[pb][64:128, :],
                               in_=st.xbcdt_d[1088:1152, tok0:tok0 + CS]).then_inc(ex, 16)
                if cc % WIN == WIN - 1:
                    w = cc // WIN
                    for o in range(KC):
                        sync.wait_ge(ewv, 8 * w + o + 1)
                        if 8 * w + o >= 1:
                            sync.wait_ge(ewd, 16 * (8 * w + o))
                        sync.dma_start(
                            out=st.out_d[o * P:(o + 1) * P, w * NT:(w + 1) * NT],
                            in_=wout[:, o % 2, :]).then_inc(ewd, 16)

        @block.gpsimd
        def _(g):
            g.wait_ge(st.ph, PH)
            ng = 0
            for k in range(KC):
                if ng:
                    g.wait_ge(egd, ng)
                g.dma_start(out=wo_sb[:, k * DM:(k + 1) * DM],
                            in_=st.w_out1[k * P:(k + 1) * P, :]).then_inc(egd, 16)
                ng += 16
            g.iota(acfih[0:32, :].rearrange("h (a i) -> h a i", a=32),
                   pattern=[[1, 32], [0, CS]], base=0, channel_multiplier=-1,
                   allow_small_or_imprecise_dtypes=True)
            g.drain()
            g.iota(acfih[64:127, :].rearrange("h (a i) -> h a i", a=32),
                   pattern=[[0, 32], [1, CS]], base=0, channel_multiplier=-1,
                   allow_small_or_imprecise_dtypes=True)
            g.drain()
            for i in range(2):
                g.iota(anb[i][64:127, 0:CS], pattern=[[1, CS]], base=0,
                       channel_multiplier=-1, allow_small_or_imprecise_dtypes=True)
                g.drain()
            g.memset(stg4[:, :], 0.0)
            g.memset(z64[:, :], 0.0)
            g.memset(anb[0][0:64, :], 0.0)
            g.memset(anb[1][0:64, :], 0.0)
            g.memset(anb[0][64:127, CS:P], 0.0)
            g.memset(anb[1][64:127, CS:P], 0.0)
            g.drain()
            g.nop().then_inc(eab, 1)
            for s in range(2):
                if s == 1:
                    g.wait_ge(esp, 2)
                if ng:
                    g.wait_ge(egd, ng)
                g.dma_start(out=dtp[:, :],
                            in_=st.dtpre_d[:, s * SEQ:(s + 1) * SEQ]).then_inc(egd, 16)
                ng += 16
                g.wait_ge(egd, ng)
                g.wait_ge(ewo, 16)
                if s == 1:
                    g.wait_ge(esp, 2)
                    g.wait_ge(epv, 2)
                g.tensor_scalar(zf[:, :], dtp[:, :], bcols[:, 0:1], 0.0,
                                ALU.add, ALU.add)
                g.drain()
                g.nop().then_inc(eab, 1)
                g.wait_ge(epv, 4 * s + 4)
                g.wait_ge(egd, ng)
                g.dma_start(out=st.ea_d[:, :], in_=ac[:, :]).then_inc(egd, 16)
                ng += 16
                g.wait_ge(egd, ng)
                if s == 1:
                    g.wait_ge(esg, 4 * NCR)
                g.dma_start(out=acfih[32:64, :].rearrange("c (h i) -> c h i", h=32),
                            in_=st.ea_d[:, :].rearrange("h (c i) -> c h i",
                                                        c=NCH)).then_inc(egd, 16)
                ng += 16
                g.wait_ge(esa, s * NCR)
                g.memset(s_sb[:, :], 0.0)
                g.drain()
                g.nop().then_inc(ept, 1)
                for c in range(NCR):
                    cc = s * NCR + c
                    pb = cc % 2
                    g.wait_ge(epv, 4 * s + 5)
                    if cc >= 2:
                        g.wait_ge(esg, 4 * (cc - 1))
                    g.tensor_copy(anb[pb][0:32, 0:CS], an_t[0:32, c * CS:(c + 1) * CS])
                    g.drain()
                    g.tensor_copy(anb[pb][32:64, :],
                                  st.ident_f[0:32, c:c + 1].broadcast_to([32, P]))
                    g.drain()
                    g.nop().then_inc(eab, 1)
                    g.wait_ge(esa, cc)
                    g.wait_ge(eym, 8 * cc)
                    if "bands" in st.e_dis:
                        g.tensor_copy(stg4[64:128, 0:1024], s_sb[:, :])
                    else:
                        g.tensor_copy(stg_bands(64, 128),
                                      s_sb[:, :].rearrange("n (h p) -> n h p", h=32))
                    g.drain()
                    g.nop().then_inc(ept, 1)
                    g.wait_ge(ede, 2 * (cc + 1))
                    g.tensor_tensor(s_t1[:, :].rearrange("n (h p) -> n h p", h=32),
                                    s_sb[:, :].rearrange("n (h p) -> n h p", h=32),
                                    cdb32[:, :].unsqueeze(2).broadcast_to([CS, 32, 32]),
                                    ALU.mult)
                    g.drain()
                    g.nop().then_inc(et1, 1)
                    g.wait_ge(es4, 2 * (cc + 1))
                    g.wait_ge(ede, 2 * cc + 1)
                    g.wait_ge(esm, 2 * cc)
                    xd_in = (stg4[0:CS, 0:1024].rearrange("j (h p) -> j h p", h=32)
                             if "bands" in st.e_dis else stg_bands(0, CS))
                    g.tensor_tensor(
                        xdec[:, :].rearrange("j (h p) -> j h p", h=32),
                        xd_in,
                        decT[:, :].unsqueeze(2).broadcast_to([CS, 32, 32]),
                        ALU.mult)
                    g.drain()
                    g.nop().then_inc(exd, 1)
                    g.wait_ge(ex, 64 * cc + 16)
                    g.wait_ge(ewo, 160)
                    g.wait_ge(eya, cc)
                    g.tensor_tensor(yv[:, :, :], xsd[pb][:, :, :],
                                    dcol[:, :].unsqueeze(2).broadcast_to([P, KC, CS]),
                                    ALU.mult)
                    g.drain()
                    g.nop().then_inc(eyv, 1)

        @block.scalar
        def _(s_):
            s_.wait_ge(st.ph, PH)
            s_.wait_ge(ewo, 32)
            s_.activation(bcols[:, 2:3], bcols[:, 1:2], AF.Exp)
            s_.drain()
            s_.nop().then_inc(esp, 1)
            for s in range(2):
                s_.wait_ge(eab, 2 + s * (1 + NCR))
                s_.drain()
                s_.activation(t32[:, :], zf[:, :], AF.Abs)
                s_.drain()
                s_.activation(t32[:, :], t32[:, :], AF.Exp, scale=-1.0)
                s_.drain()
                s_.activation(t32[:, :], t32[:, :], AF.Ln,
                              bias=st.onecol[0:32, 0:1], scale=1.0)
                s_.drain()
                s_.activation(zf[:, :], zf[:, :], AF.Relu)
                s_.drain()
                s_.nop().then_inc(esp, 1)
                for k in range(KC):
                    s_.wait_ge(etr, 64 * s + 4 * (k + 1))
                    if s > 0 and k == 0:
                        s_.wait_ge(es4, 2 * NCR * s)
                    s_.drain()
                    s_.activation(dtT[:, 128 * k:128 * (k + 1)], ptr_d(), AF.Copy)
                    s_.drain()
                    s_.nop().then_inc(eta, 1)
                for k in range(KC):
                    s_.wait_ge(etr, 64 * s + 32 + 4 * (k + 1))
                    if s > 0 and k == 0:
                        s_.wait_ge(eds, NCR * s)
                    s_.drain()
                    s_.activation(acT[:, 128 * k:128 * (k + 1)], ptr_a(), AF.Copy)
                    s_.drain()
                    s_.nop().then_inc(eta, 1)
                for c in range(NCR):
                    cc = s * NCR + c
                    s_.wait_ge(esg, 4 * cc + 2)
                    s_.wait_ge(emh, 2 * cc)
                    s_.drain()
                    s_.activation(lm[:, 0:1024], ps_seg[0][:, :], AF.Exp)
                    s_.drain()
                    s_.nop().then_inc(exp_, 1)
                    s_.wait_ge(esg, 4 * cc + 4)
                    s_.drain()
                    s_.activation(lm[:, 1024:2048], ps_seg[1][:, :], AF.Exp)
                    s_.drain()
                    s_.nop().then_inc(exp_, 1)
                    s_.wait_ge(eds, cc + 1)
                    s_.drain()
                    s_.activation(decT[:, :], decT[:, :], AF.Exp)
                    s_.drain()
                    s_.nop().then_inc(ede, 1)
                    s_.wait_ge(edm, 2 * (cc + 1))
                    s_.wait_ge(et1, cc)
                    s_.drain()
                    s_.activation(cdb32[:, :], ps_dc(), AF.Exp)
                    s_.drain()
                    s_.nop().then_inc(ede, 1)

        @block.vector
        def _(v):
            v.wait_ge(st.ph, PH)
            v.wait_ge(eab, 1)
            v.tensor_single_scalar(acfih[0:32, :], acfih[0:32, :], 0.0, ALU.is_equal)
            v.drain()
            v.tensor_single_scalar(acfih[64:127, :], acfih[64:127, :], 0.0,
                                   ALU.is_equal)
            v.drain()
            for i in range(2):
                v.tensor_scalar(anb[i][64:127, 0:CS], anb[i][64:127, 0:CS],
                                0.0, -1e9, ALU.is_gt, ALU.mult)
                v.drain()
            v.wait_ge(esp, 1)
            v.tensor_scalar(bcols[:, 3:4], bcols[:, 2:3], -1.0, 0.0,
                            ALU.mult, ALU.add)
            v.drain()
            v.nop().then_inc(epv, 1)
            for s in range(2):
                v.wait_ge(esp, 2 + s)
                if s:
                    v.wait_ge(eta, 16 * s)
                    v.wait_ge(edm, 2 * NCR * s)
                v.drain()
                v.tensor_tensor(dt_f[:, :], zf[:, :], t32[:, :], ALU.add)
                v.drain()
                v.nop().then_inc(epv, 1)
                v.tensor_scalar(t32[:, :], dt_f[:, :], bcols[:, 3:4], 0.0,
                                ALU.mult, ALU.add)
                v.drain()
                v.nop().then_inc(epv, 1)
                for c in range(NCH):
                    v.tensor_tensor_scan(ac[:, c * CS:(c + 1) * CS],
                                         t32[:, c * CS:(c + 1) * CS], z64[:, :],
                                         0.0, ALU.add, ALU.add)
                    v.drain()
                v.nop().then_inc(epv, 1)
                v.tensor_scalar(an_t[:, :], ac[:, :], -1.0, 0.0, ALU.mult, ALU.add)
                v.drain()
                v.nop().then_inc(epv, 1)
                for c in range(NCR):
                    cc = s * NCR + c
                    pb = cc % 2
                    v.wait_ge(edm, 2 * cc + 1)
                    v.drain()
                    v.tensor_copy(acE[0:1, :], ps_ae())
                    v.drain()
                    v.nop().then_inc(eav, 1)
                    v.wait_ge(etx, 9 * cc + 4)
                    v.wait_ge(eym, 8 * cc)
                    v.wait_ge(eta, 16 * s + KC)
                    v.drain()
                    for half in range(2):
                        if half == 1:
                            v.wait_ge(etx, 9 * cc + 8)
                        tt_out = (stg4[0:CS, 512 * half:512 * (half + 1)]
                                  .rearrange("j (h p) -> j h p", h=16)
                                  if "bands" in st.e_dis else
                                  stg_bands(0, CS, 16 * half, 16 * (half + 1)))
                        v.tensor_tensor(
                            tt_out,
                            ps_tb[0:CS, 0:512].rearrange("j (h p) -> j h p", h=16),
                            dtT[:, c * 32 + 16 * half:c * 32 + 16 * (half + 1)]
                            .unsqueeze(2).broadcast_to([CS, 16, 32]),
                            ALU.mult)
                        v.drain()
                        v.nop().then_inc(es4, 1)
                    v.wait_ge(etx, 9 * cc + 9)
                    v.wait_ge(esm, 2 * cc)
                    v.drain()
                    v.tensor_copy(btT[pb][:, :], ps_tb[0:CS, 512:576])
                    v.drain()
                    v.nop().then_inc(ebt, 1)
                    v.wait_ge(ecb, cc + 1)
                    v.wait_ge(emh, 2 * (cc - 1) if cc >= 2 else 0)
                    v.drain()
                    v.tensor_copy(cbc[pb][0:CS, :], ps_cb())
                    v.drain()
                    v.nop().then_inc(ecv, 1)
                    v.wait_ge(edm, 2 * (cc + 1))
                    v.wait_ge(exd, cc)
                    v.drain()
                    v.tensor_tensor(decT[:, :], ps_dc(), acT[:, c * 32:(c + 1) * 32],
                                    ALU.subtract)
                    v.drain()
                    v.nop().then_inc(eds, 1)
                    for half in range(2):
                        v.wait_ge(exp_, 2 * cc + half + 1)
                        v.wait_ge(ecv, cc + 1)
                        v.wait_ge(ex, 64 * cc + 64)
                        v.wait_ge(eym, 8 * cc)
                        v.drain()
                        v.tensor_tensor(
                            msb[:, 1024 * half:1024 * (half + 1)]
                            .rearrange("p (h i) -> p h i", h=16),
                            lm[:, 1024 * half:1024 * (half + 1)]
                            .rearrange("p (h i) -> p h i", h=16),
                            cbc[pb][:, :].unsqueeze(1).broadcast_to([P, 16, CS]),
                            ALU.mult)
                        v.drain()
                        v.nop().then_inc(emh, 1)
                    v.wait_ge(eym, 8 * (cc + 1))
                    v.wait_ge(eyv, cc + 1)
                    if cc % WIN == 0 and cc >= WIN:
                        v.wait_ge(ewm, 8 * (cc // WIN))
                    v.drain()
                    v.tensor_tensor(
                        yw[:, :, (cc % WIN) * CS:(cc % WIN) * CS + CS],
                        yv[:, :, :],
                        ps_y[cc % 2][:, :].rearrange("p (g x) -> p g x", g=KC),
                        ALU.add)
                    v.drain()
                    v.nop().then_inc(eya, 1)
                    v.wait_ge(et1, cc + 1)
                    v.wait_ge(ept, 2 + cc + s)
                    for half in range(2):
                        v.wait_ge(esm, 2 * cc + half + 1)
                        v.drain()
                        v.tensor_tensor(s_sb[:, 512 * half:512 * (half + 1)],
                                        s_t1[:, 512 * half:512 * (half + 1)],
                                        ps_st[half](), ALU.add)
                        v.drain()
                    v.nop().then_inc(esa, 1)
                    if cc % WIN == WIN - 1:
                        w = cc // WIN
                        for o in range(KC):
                            v.wait_ge(ewm, 8 * w + o + 1)
                            if 8 * w + o >= 2:
                                v.wait_ge(ewd, 16 * (8 * w + o - 1))
                            v.drain()
                            v.tensor_copy(wout[:, o % 2, :], ps_w[o % 2]())
                            v.drain()
                            v.nop().then_inc(ewv, 1)

        @block.tensor
        def _(tns):
            tns.wait_ge(st.ph, PH)
            for s in range(2):
                tns.wait_ge(epv, 4 * s + 2)
                for k in range(KC):
                    if s > 0 and k == 0:
                        tns.wait_ge(ecv, NCR * s)
                        tns.wait_ge(eds, NCR * s)
                        tns.wait_ge(ede, 2 * NCR * s)
                        tns.wait_ge(eav, NCR * s)
                    if s > 0 or k > 0:
                        tns.wait_ge(eta, 16 * s + k)
                    for j in range(4):
                        c = 4 * k + j
                        tns.transpose(ptr_d()[:, 32 * j:32 * (j + 1)],
                                      dt_f[:, c * CS:(c + 1) * CS],
                                      st.ident_f[0:32, 0:32]).then_inc(etr, 1)
                tns.wait_ge(epv, 4 * s + 4)
                for k in range(KC):
                    if k == 0:
                        tns.wait_ge(eta, 16 * s + KC)
                    if s > 0 or k > 0:
                        tns.wait_ge(eta, 16 * s + KC + k)
                    for j in range(4):
                        c = 4 * k + j
                        tns.transpose(ptr_a()[:, 32 * j:32 * (j + 1)],
                                      ac[:, c * CS:(c + 1) * CS],
                                      st.ident_f[0:32, 0:32]).then_inc(etr, 1)
                for c in range(NCR):
                    cc = s * NCR + c
                    pb = cc % 2
                    w = cc // WIN
                    tns.wait_ge(eav, cc)
                    tns.wait_ge(ecv, cc)
                    tns.wait_ge(eds, cc)
                    tns.wait_ge(ede, 2 * cc)
                    tns.wait_ge(eta, 16 * (s + 1))
                    tns.transpose(ps_ae(), ac[:, c * CS + CS - 1:c * CS + CS],
                                  st.ident_f[0:32, 0:32]).then_inc(edm, 1)
                    tns.wait_ge(eab, 1 + (s + 1) + cc + 1)
                    tns.wait_ge(egd, 128 + 48 * (s + 1))
                    if cc >= 1:
                        tns.wait_ge(exp_, 2 * cc - 1)
                    if cc % WIN == 0 and cc >= WIN:
                        tns.wait_ge(ewv, 8 * w)
                    for half2 in range(2):
                        tns.matmul(ps_seg[0][:, 512 * half2:512 * (half2 + 1)],
                                   anb[pb][:, :],
                                   acfih[:, 512 * half2:512 * (half2 + 1)],
                                   start=True, stop=True).then_inc(esg, 1)
                    tns.wait_ge(ex, 64 * cc + 16)
                    tns.wait_ge(es4, 2 * cc)
                    tns.wait_ge(ebt, cc)
                    for g in range(4):
                        tns.transpose(ps_tb[0:CS, 128 * g:128 * (g + 1)],
                                      xsd[pb][:, g, :],
                                      st.ident_bf[:, :]).then_inc(etx, 1)
                    if cc >= 1:
                        tns.wait_ge(exp_, 2 * cc)
                        tns.wait_ge(esa, cc)
                    for half2 in range(2):
                        tns.matmul(ps_seg[1][:, 512 * half2:512 * (half2 + 1)],
                                   anb[pb][:, :],
                                   acfih[:, 1024 + 512 * half2:1024 + 512 * (half2 + 1)],
                                   start=True, stop=True).then_inc(esg, 1)
                    tns.wait_ge(es4, 2 * cc + 1)
                    for g in range(4):
                        tns.transpose(ps_tb[0:CS, 128 * g:128 * (g + 1)],
                                      xsd[pb][:, 4 + g, :],
                                      st.ident_bf[:, :]).then_inc(etx, 1)
                    tns.wait_ge(ex, 64 * cc + 32)
                    tns.wait_ge(ebt, cc)
                    tns.wait_ge(es4, 2 * cc + 2)
                    tns.transpose(ps_tb[0:CS, 512:576], b_sb[pb][:, :],
                                  st.ident_bf[0:CS, 0:CS]).then_inc(etx, 1)
                    tns.wait_ge(ex, 64 * cc + 48)
                    tns.wait_ge(ecv, cc)
                    tns.wait_ge(eav, cc + 1)
                    tns.matmul(ps_cb(), b_sb[pb][:, :], c_sb[pb][:, :],
                               start=True, stop=True).then_inc(ecb, 1)
                    tns.wait_ge(eav, cc + 1)
                    tns.wait_ge(ede, 2 * cc)
                    tns.wait_ge(eds, cc)
                    tns.wait_ge(ecv, cc + 1)
                    tns.matmul(ps_dc(), st.ones_f[0:1, 0:CS], acE[0:1, :],
                               start=True, stop=True).then_inc(edm, 1)
                    tns.wait_ge(es4, 2 * (cc + 1))
                    tns.wait_ge(ept, 2 + cc + s)
                    tns.wait_ge(emh, 2 * (cc + 1))
                    if cc >= 2:
                        tns.wait_ge(eya, cc - 1)
                    for h in range(32):
                        g2, r = h // 4, h % 4
                        w0 = 0 if "y" in st.e_dis else 256 * h - 32 * r
                        inst = tns.matmul(ps_y[cc % 2][:, CS * g2:CS * (g2 + 1)],
                                          stg4[:, w0:w0 + 128],
                                          msb[:, CS * h:CS * h + CS],
                                          start=(r == 0), stop=(r == 3))
                        if r == 3:
                            inst.then_inc(eym, 1)
                    tns.wait_ge(ebt, cc + 1)
                    tns.wait_ge(exd, cc + 1)
                    tns.wait_ge(exp_, 2 * cc + 2)
                    tns.wait_ge(esa, cc)
                    for half2 in range(2):
                        tns.matmul(ps_st[half2](), btT[pb][:, :],
                                   xdec[:, 512 * half2:512 * (half2 + 1)],
                                   start=True, stop=True).then_inc(esm, 1)
                    if cc % WIN == WIN - 1:
                        tns.wait_ge(eya, cc + 1)
                        tns.wait_ge(egd, 128)
                        tns.wait_ge(exp_, 2 * cc + 1)
                        for o in range(KC):
                            if 8 * w + o >= 2:
                                tns.wait_ge(ewv, 8 * w + o - 1)
                            for k in range(KC):
                                inst = tns.matmul(ps_w[o % 2](),
                                                  wo_sb[:, k * DM + o * P:
                                                        k * DM + (o + 1) * P],
                                                  yw[:, k, :],
                                                  start=(k == 0), stop=(k == KC - 1))
                            inst.then_inc(ewm, 1)

        barrier(st, [(ewd, 16 * KC * (2 * NCR // WIN))])


def _pass_cols(st, PE, nc, names):
    return PE(nc.sbuf_tensor(names, [P, 8], F32))


def phase_f(st):
    nc, block, E, sem = st.nc, st.block, st.E, st.sem
    PH = st.phase_idx
    fx = sem("f_x"); fv = sem("f_v"); fs = sem("f_s"); fdb = sem("f_db")
    fcc = sem("f_cc"); ffin = sem("f_fin"); fsl = sem("f_sl")
    with contextlib.ExitStack() as ps:
        PE = ps.enter_context
        gt_sb = [PE(nc.sbuf_tensor("f_gt0", [P, 24, NT], BF16))] * 2
        ki = PE(nc.sbuf_tensor("f_ki", [P, 24 * NT], F32))
        gq = PE(nc.sbuf_tensor("f_gq", [P, 24 * NT], F32))
        ks = PE(nc.sbuf_tensor("f_ks", [P, 24 * NT], U16))
        pk = PE(nc.sbuf_tensor("f_pk", [P, 24 * NT], F32))
        scol = PE(nc.sbuf_tensor("f_scol", [P, 2], F32))
        mxc = PE(nc.sbuf_tensor("f_mx", [P, 2], F32))
        mrow_b = PE(nc.sbuf_tensor("f_mrow", [1, P], F32))
        cells = PE(nc.sbuf_tensor("f_cells", [1, 16], F32))
        ccs = PE(nc.sbuf_tensor("f_ccs", [1, 16], F32))
        c655 = PE(nc.sbuf_tensor("f_c655", [1, 1], F32))

        gt_t = st.gate1_d.rearrange("(g q) t -> q g t", q=P)

        @block.sync
        def _(sync):
            sync.wait_ge(st.ph, PH)
            sync.dma_start(out=scol[:, 0:1],
                           in_=st.scl_d[0:1, 17:18].broadcast_to([P, 1])).then_inc(fsl, 16)
            sync.wait_ge(fsl, 16)
            sync.dma_start(out=scol[:, 1:2],
                           in_=st.scl_d[0:1, 19:20].broadcast_to([P, 1])).then_inc(fsl, 16)
            for t in range(NTILES):
                if t >= 1:
                    sync.wait_ge(fx, 16 * t)
                    sync.wait_ge(sem("f_gq"), t)
                sync.dma_start(out=gt_sb[t % 2][:, :, :],
                               in_=gt_t[:, :, t * NT:(t + 1) * NT]).then_inc(fx, 16)

        @block.vector
        def _(v):
            v.wait_ge(st.ph, PH)
            v.wait_ge(fsl, 32)
            v.memset(mxc[:, :], 0.0)
            v.memset(c655[0:1, 0:1], 1.52587890625e-05)
            for t in range(NTILES):
                b = t % 2
                v.wait_ge(fx, 16 * (t + 1))
                v.drain()
                flat = gt_sb[b][:, :, :].rearrange("p a b -> p (a b)")
                v.tensor_scalar(pk[:, :], flat, scol[:, 1:2], MAGIC, ALU.mult, ALU.add)
                v.drain()
                v.tensor_scalar(ki[:, :], pk[:, :], MAGIC, 0.0, ALU.subtract, ALU.add)
                v.drain()
                v.tensor_scalar(gq[:, :], ki[:, :], scol[:, 0:1], 0.0,
                                ALU.mult, ALU.add).then_inc(sem("f_gq"), 1)
                v.drain()
                v.wait_ge(fs, t + 1)
                v.tensor_scalar(gq[:, :], gq[:, :], 65536.0, MAGIC, ALU.mult, ALU.add)
                v.drain()
                v.tensor_scalar(ks[:, :], gq[:, :], MAGIC, 65535.0, ALU.subtract, ALU.min)
                v.drain()
                v.tensor_tensor(pk[:, :], ki[:, :], ks[:, :], ALU.mult)
                v.drain()
                v.tensor_reduce(mxc[:, 1:2], pk[:, :], axis=AX.X, op=ALU.max,
                                apply_absolute_value=True)
                v.drain()
                inst = v.tensor_tensor(mxc[:, 0:1], mxc[:, 0:1], mxc[:, 1:2], ALU.max)
                inst.then_inc(fv, 1)
                v.drain()
            v.wait_ge(fdb, 32)
            v.tensor_reduce(cells[0:1, 3:4], mrow_b[0:1, :], axis=AX.X,
                            op=ALU.max)
            v.drain()
            v.memset(ccs[0:1, :], 0.0)
            v.drain()
            v.tensor_tensor(ccs[0:1, 3:4], cells[0:1, 3:4], scol[0:1, 0:1], ALU.mult)
            v.drain()
            v.tensor_tensor(ccs[0:1, 3:4], ccs[0:1, 3:4], c655[0:1, 0:1],
                            ALU.mult).then_inc(ffin, 1)
            v.drain()
            v.wait_ge(fdb, 64)
            v.tensor_tensor(ccs[0:1, 8:9], cells[0:1, 3:4], st.c327[0:1, 1:2], ALU.mult)
            v.drain()
            v.tensor_scalar(ccs[0:1, 8:9], ccs[0:1, 8:9], 1e-12, 0.0, ALU.max, ALU.add)
            v.drain()
            v.reciprocal(ccs[0:1, 9:10], ccs[0:1, 8:9])
            v.drain()
            v.tensor_tensor(ccs[0:1, 10:11], scol[0:1, 0:1], c655[0:1, 0:1], ALU.mult)
            v.drain()
            v.tensor_tensor(ccs[0:1, 10:11], ccs[0:1, 10:11], ccs[0:1, 9:10],
                            ALU.mult).then_inc(ffin, 1)
            v.drain()

        @block.scalar
        def _(s):
            s.wait_ge(st.ph, PH)
            for t in range(NTILES):
                s.wait_ge(sem("f_gq"), t + 1)
                s.drain()
                s.activation(gq[:, :], gq[:, :], AF.Sigmoid).then_inc(fs, 1)
                s.drain()

        @block.gpsimd
        def _(g):
            g.wait_ge(st.ph, PH)
            g.wait_ge(fv, NTILES)
            g.dma_start(out=st.mb_d[3:4, :], in_=mxc[:, 0:1]).then_inc(fdb, 16)
            g.wait_ge(fdb, 16)
            g.dma_start(out=mrow_b[0:1, :], in_=st.mb_d[3:4, :]).then_inc(fdb, 16)
            g.wait_ge(ffin, 1)
            g.dma_start(out=st.cc_in_d[:, :], in_=ccs[0:1, :]).then_inc(fdb, 16)
            g.wait_ge(fdb, 48)
            g.collective_compute(
                "AllReduce", ALU.max,
                replica_groups=[[0, 1, 2, 3, 4, 5, 6, 7]],
                ins=[st.cc_in_d[:, :]], outs=[st.cc_out_ds[2][:, :]],
            ).then_inc(fcc, 1)
            g.wait_ge(fcc, 1)
            g.dma_start(out=cells[0:1, :], in_=st.cc_out_ds[2][:, :]).then_inc(fdb, 16)
            g.wait_ge(ffin, 2)
            g.dma_start(out=st.scl_d[0:1, 20:23], in_=ccs[0:1, 8:11]).then_inc(fdb, 16)

        barrier(st, [(fdb, 80)])


def phase_g(st):
    nc, block, E, sem = st.nc, st.block, st.E, st.sem
    PH = st.phase_idx
    gx = sem("g_x"); gu = sem("g_u"); gv = sem("g_v"); gs = sem("g_s")
    ggq = sem("g_gq"); gdb = sem("g_db"); gcc = sem("g_cc"); gfin = sem("g_fin")
    gsl = sem("g_sl"); gst = sem("g_st")
    with contextlib.ExitStack() as ps:
        PE = ps.enter_context
        gt_sb = [PE(nc.sbuf_tensor("g_gt0", [P, 24, NT], BF16))] * 2
        up_sb = [PE(nc.sbuf_tensor("g_up0", [P, 24, NT], BF16))] * 2
        ki = PE(nc.sbuf_tensor("g_ki", [P, 24 * NT], I16))
        ku = PE(nc.sbuf_tensor("g_ku", [P, 24 * NT], I16))
        gq = PE(nc.sbuf_tensor("g_gq", [P, 24 * NT], F32))
        pk = gq
        po = PE(nc.sbuf_tensor("g_po", [P, 24, NT], BF16))
        ks_ap = po[:, :, :].rearrange("p a b -> p (a b)").bitcast(U16)
        scol = PE(nc.sbuf_tensor("g_scol", [P, 4], F32))
        srow = PE(nc.sbuf_tensor("g_srow", [1, 32], F32))
        mxc = PE(nc.sbuf_tensor("g_mx", [P, 2], F32))
        mrow_b = PE(nc.sbuf_tensor("g_mrow", [1, P], F32))
        cells = PE(nc.sbuf_tensor("g_cells", [1, 16], F32))
        ccs = PE(nc.sbuf_tensor("g_ccs", [1, 16], F32))

        gt_t = st.gate1_d.rearrange("(g q) t -> q g t", q=P)
        up_t = st.up_d.rearrange("(g q) t -> q g t", q=P)
        o_t = st.o_d.rearrange("(g q) t -> q g t", q=P)

        @block.sync
        def _(sync):
            sync.wait_ge(st.ph, PH)
            for i, slot in enumerate([17, 19, 22, 18]):
                if i:
                    sync.wait_ge(gsl, 16 * i)
                sync.dma_start(out=scol[:, i:i + 1],
                               in_=st.scl_d[0:1, slot:slot + 1].broadcast_to([P, 1])).then_inc(gsl, 16)
            sync.wait_ge(gsl, 64)
            sync.dma_start(out=srow[0:1, 0:7], in_=st.scl_d[0:1, 16:23]).then_inc(gsl, 16)
            for t in range(NTILES):
                if t >= 1:
                    sync.wait_ge(gx, 16 * t)
                    sync.wait_ge(ggq, t)
                sync.dma_start(out=gt_sb[t % 2][:, :, :],
                               in_=gt_t[:, :, t * NT:(t + 1) * NT]).then_inc(gx, 16)
                if t >= 1:
                    sync.wait_ge(gu, 16 * t)
                    sync.wait_ge(sem("g_kuv"), t)
                sync.dma_start(out=up_sb[t % 2][:, :, :],
                               in_=up_t[:, :, t * NT:(t + 1) * NT]).then_inc(gu, 16)

        @block.vector
        def _(v):
            v.wait_ge(st.ph, PH)
            v.wait_ge(gsl, 80)
            v.memset(mxc[:, :], 0.0)
            for t in range(NTILES):
                b = t % 2
                v.wait_ge(gx, 16 * (t + 1))
                v.wait_ge(gu, 16 * (t + 1))
                if t >= 1:
                    v.wait_ge(gst, 16 * t)
                v.drain()
                gflat = gt_sb[b][:, :, :].rearrange("p a b -> p (a b)")
                uflat = up_sb[b][:, :, :].rearrange("p a b -> p (a b)")
                v.tensor_scalar(pk[:, :], gflat, scol[:, 1:2], MAGIC, ALU.mult, ALU.add)
                v.drain()
                v.tensor_scalar(ki[:, :], pk[:, :], MAGIC, 0.0, ALU.subtract, ALU.add)
                v.drain()
                v.tensor_scalar(gq[:, :], ki[:, :], scol[:, 0:1], 0.0,
                                ALU.mult, ALU.add).then_inc(ggq, 1)
                v.drain()
                v.wait_ge(gs, t + 1)
                v.tensor_scalar(gq[:, :], gq[:, :], 65536.0, MAGIC, ALU.mult, ALU.add)
                v.drain()
                v.tensor_scalar(ks_ap, gq[:, :], MAGIC, 65535.0, ALU.subtract, ALU.min)
                v.drain()
                v.tensor_scalar(pk[:, :], uflat, scol[:, 3:4], MAGIC, ALU.mult, ALU.add)
                v.drain()
                v.tensor_scalar(ku[:, :], pk[:, :], MAGIC, 0.0,
                                ALU.subtract, ALU.add).then_inc(sem("g_kuv"), 1)
                v.drain()
                v.tensor_tensor(pk[:, :], ki[:, :], ks_ap, ALU.mult)
                v.drain()
                v.tensor_scalar(pk[:, :], pk[:, :], scol[:, 2:3], MAGIC, ALU.mult, ALU.add)
                v.drain()
                v.tensor_scalar(ki[:, :], pk[:, :], MAGIC, 0.0, ALU.subtract, ALU.add)
                v.drain()
                v.tensor_tensor(po[:, :, :].rearrange("p a b -> p (a b)"), ki[:, :],
                                ku[:, :], ALU.mult).then_inc(gv, 1)
                v.drain()
                v.tensor_reduce(mxc[:, 1:2], po[:, :, :].rearrange("p a b -> p (a b)"),
                                axis=AX.X, op=ALU.max, apply_absolute_value=True)
                v.drain()
                v.tensor_tensor(mxc[:, 0:1], mxc[:, 0:1], mxc[:, 1:2],
                                ALU.max).then_inc(gv, 1)
                v.drain()
            v.wait_ge(gdb, 32)
            v.tensor_reduce(cells[0:1, 4:5], mrow_b[0:1, :], axis=AX.X, op=ALU.max)
            v.drain()
            v.memset(ccs[0:1, :], 0.0)
            v.drain()
            v.tensor_tensor(ccs[0:1, 6:7], srow[0:1, 4:5], srow[0:1, 0:1], ALU.mult)
            v.drain()
            v.tensor_tensor(ccs[0:1, 4:5], cells[0:1, 4:5], ccs[0:1, 6:7],
                            ALU.mult).then_inc(gfin, 1)
            v.drain()
            v.wait_ge(gdb, 64)
            v.tensor_tensor(ccs[0:1, 12:13], cells[0:1, 4:5], st.c327[0:1, 1:2], ALU.mult)
            v.drain()
            v.tensor_scalar(ccs[0:1, 12:13], ccs[0:1, 12:13], 1e-12, 0.0, ALU.max, ALU.add)
            v.drain()
            v.reciprocal(ccs[0:1, 14:15], ccs[0:1, 12:13])
            v.drain()
            v.tensor_tensor(ccs[0:1, 13:14], ccs[0:1, 6:7], ccs[0:1, 14:15],
                            ALU.mult).then_inc(gfin, 1)
            v.drain()

        @block.scalar
        def _(s):
            s.wait_ge(st.ph, PH)
            for t in range(NTILES):
                s.wait_ge(ggq, t + 1)
                s.drain()
                s.activation(gq[:, :], gq[:, :], AF.Sigmoid).then_inc(gs, 1)
                s.drain()
                s.wait_ge(gv, 2 * t + 1)
                if t >= 1:
                    s.wait_ge(gst, 16 * t)
                s.drain()
                s.dma_start(out=o_t[:, :, t * NT:(t + 1) * NT],
                            in_=po[:, :, :]).then_inc(gst, 16)

        @block.gpsimd
        def _(g):
            g.wait_ge(st.ph, PH)
            g.wait_ge(gv, 2 * NTILES)
            g.dma_start(out=st.mb_d[4:5, :], in_=mxc[:, 0:1]).then_inc(gdb, 16)
            g.wait_ge(gdb, 16)
            g.dma_start(out=mrow_b[0:1, :], in_=st.mb_d[4:5, :]).then_inc(gdb, 16)
            g.wait_ge(gfin, 1)
            g.dma_start(out=st.cc_in_d[:, :], in_=ccs[0:1, :]).then_inc(gdb, 16)
            g.wait_ge(gdb, 48)
            g.collective_compute(
                "AllReduce", ALU.max,
                replica_groups=[[0, 1, 2, 3, 4, 5, 6, 7]],
                ins=[st.cc_in_d[:, :]], outs=[st.cc_out_ds[3][:, :]],
            ).then_inc(gcc, 1)
            g.wait_ge(gcc, 1)
            g.dma_start(out=cells[0:1, :], in_=st.cc_out_ds[3][:, :]).then_inc(gdb, 16)
            g.wait_ge(gfin, 2)
            g.dma_start(out=st.scl_d[0:1, 23:25], in_=ccs[0:1, 12:14]).then_inc(gdb, 16)

        barrier(st, [(gdb, 80), (gst, 16 * NTILES)])


def phase_h(st):
    nc, block, E, sem = st.nc, st.block, st.E, st.sem
    PH = st.phase_idx
    hx = sem("h_x"); hp = sem("h_p"); hmm = sem("h_mm"); hev = sem("h_ev")
    hst = sem("h_st"); hq = sem("h_q"); hsl = sem("h_sl")
    with contextlib.ExitStack() as ps:
        PE = ps.enter_context
        wq_dn = PE(nc.sbuf_tensor("h_wd", [P, 24 * DM], BF16))
        with contextlib.ExitStack() as dqs_scope:
            dequant_weight(st, dqs_scope, "dn", st.w_down_n, st.w_down_t, wq_dn, 96, 24, KC)
        po_sb = [PE(nc.sbuf_tensor(f"h_po{i}", [P, 24, NT], BF16)) for i in range(2)]
        xp_sb = [PE(nc.sbuf_tensor("h_xp0", [P, KC, NT], F32))] * 2
        ki = PE(nc.sbuf_tensor("h_ki", [P, 24 * NT], F32))
        oq = PE(nc.sbuf_tensor("h_oq", [P, 24, NT], BF16))
        out_sb = [PE(nc.sbuf_tensor("h_out0", [P, KC, NT], F32))] * 2
        scol = PE(nc.sbuf_tensor("h_scol", [P, 2], F32))
        pmm = [PE(nc.psum_tensor(f"h_p{i}", [P, NT], F32)) for i in range(2)]

        po_t = st.o_d.rearrange("(g q) t -> q g t", q=P)
        x2_t = st.x1_d.rearrange("(g q) t -> q g t", q=P)
        out_t = st.out_d.rearrange("(g q) t -> q g t", q=P)

        @block.sync
        def _(sync):
            sync.wait_ge(st.ph, PH)
            sync.wait_ge(sem("dqv_dn"), 24 + KC)
            sync.dma_start(out=scol[:, 0:1],
                           in_=st.scl_d[0:1, 23:24].broadcast_to([P, 1])).then_inc(hsl, 16)
            sync.wait_ge(hsl, 16)
            sync.dma_start(out=scol[:, 1:2],
                           in_=st.scl_d[0:1, 24:25].broadcast_to([P, 1])).then_inc(hsl, 16)
            for t in range(NTILES):
                if t >= 1:
                    sync.wait_ge(hx, 16 * t)
                if t >= 2:
                    sync.wait_ge(hq, t - 1)
                sync.dma_start(out=po_sb[t % 2][:, :, :],
                               in_=po_t[:, :, t * NT:(t + 1) * NT]).then_inc(hx, 16)
                if t >= 1:
                    sync.wait_ge(hp, 16 * t)
                    sync.wait_ge(hev, KC * t)
                sync.dma_start(out=xp_sb[t % 2][:, :, :],
                               in_=x2_t[:, :, t * NT:(t + 1) * NT]).then_inc(hp, 16)

        @block.vector
        def _(v):
            v.wait_ge(st.ph, PH)
            v.wait_ge(hsl, 32)
            for t in range(NTILES):
                b = t % 2
                v.wait_ge(hx, 16 * (t + 1))
                if t >= 1:
                    v.wait_ge(hmm, KC * t)
                v.drain()
                v.tensor_scalar(ki[:, :], po_sb[b][:, :, :].rearrange("p a b -> p (a b)"),
                                scol[:, 1:2], MAGIC, ALU.mult, ALU.add)
                v.drain()
                v.tensor_scalar(oq[:, :, :].rearrange("p a b -> p (a b)"), ki[:, :],
                                MAGIC, scol[:, 0:1], ALU.subtract, ALU.mult).then_inc(hq, 1)
                v.drain()
                v.wait_ge(hp, 16 * (t + 1))
                if t >= 1:
                    v.wait_ge(hst, 16 * t)
                for o in range(KC):
                    go = t * KC + o
                    v.wait_ge(hmm, go + 1)
                    v.tensor_tensor(out_sb[b][:, o, :], pmm[go % 2][:, :],
                                    xp_sb[b][:, o, :], ALU.add).then_inc(hev, 1)
                    v.drain()

        @block.tensor
        def _(tns):
            tns.wait_ge(st.ph, PH)
            tns.wait_ge(sem("dqv_dn"), KC + 24)
            for t in range(NTILES):
                tns.wait_ge(hq, t + 1)
                for o in range(KC):
                    go = t * KC + o
                    if go >= 2:
                        tns.wait_ge(hev, go - 1)
                    for k in range(24):
                        inst = tns.matmul(pmm[go % 2][:, :],
                                          wq_dn[:, k * DM + o * P:k * DM + (o + 1) * P],
                                          oq[:, k, :],
                                          start=(k == 0), stop=(k == 23))
                    inst.then_inc(hmm, 1)

        @block.gpsimd
        def _(g):
            g.wait_ge(st.ph, PH)
            for t in range(NTILES):
                g.wait_ge(hev, KC * (t + 1))
                if t >= 1:
                    g.wait_ge(hst, 16 * t)
                g.dma_start(out=out_t[:, :, t * NT:(t + 1) * NT],
                            in_=out_sb[t % 2][:, :, :],
                            accum_op=ALU.add).then_inc(hst, 16)

        barrier(st, [(hst, 16 * NTILES)])


def host_in_maps(I):
    I = {k: np.asarray(v) for k, v in I.items()}

    def col(v, parts):
        return np.ascontiguousarray(v.reshape(parts, P).T.astype(np.float32))

    com = {
        "w_in": I["L0_attn_W_in"].astype(np.float32),
        "b_in": col(I["L0_attn_b_in"], 32),
        "convw": np.ascontiguousarray(
            I["L0_attn_conv_w"].reshape(GC, P, 3).transpose(1, 0, 2).reshape(P, GC * 3)).astype(np.float32),
        "convb": col(I["L0_attn_conv_b"], GC),
        "w_dbc": np.concatenate([I["L0_attn_W_dt"], I["L0_attn_W_B"], I["L0_attn_W_C"]],
                                axis=1).astype(np.float32),
        "b_dbc": np.stack([I["L0_attn_b_dt"], I["L0_attn_b_B"], I["L0_attn_b_C"]],
                          axis=1).astype(np.float32),
        "w_si": I["L0_attn_W_si"].astype(np.float32),
        "b_si_row": I["L0_attn_b_si"].reshape(1, INNER).astype(np.float32),
        "b_out0_row": I["L0_attn_b_out"].reshape(1, DM).astype(np.float32),
        "b_si": col(I["L0_attn_b_si"], GC),
        "d0": col(I["L0_attn_D"], GC),
        "w_out0": I["L0_attn_W_out"].astype(np.float32),
        "b_out0": col(I["L0_attn_b_out"], KC),
        "w_rkv": np.concatenate([I["L0_ffn_W_r"].T, I["L0_ffn_W_k"].T, I["L0_ffn_W_v"].T],
                                axis=1).astype(np.float32),
        "mu_rk": np.concatenate([col(I["L0_ffn_mu_r"], KC), col(I["L0_ffn_mu_k"], KC)],
                                axis=1).astype(np.float32),
        "w_in1": np.ascontiguousarray(I["L1_attn_W_in"].T).astype(np.float32),
        "dtb1": I["L1_attn_dt_bias"].reshape(32, 1).astype(np.float32),
        "alog1": I["L1_attn_A_log"].reshape(32, 1).astype(np.float32),
        "d1": I["L1_attn_D"].reshape(32, 1).astype(np.float32),
        "w_out1": np.ascontiguousarray(I["L1_attn_W_out"].T).astype(np.float32),
        "w_up_t": np.ascontiguousarray(I["L1_ffn_w_up"].T).astype(np.float32),
        "w_gate_t": np.ascontiguousarray(I["L1_ffn_w_gate"].T).astype(np.float32),
        "w_down_t": np.ascontiguousarray(I["L1_ffn_w_down"].T).astype(np.float32),
        "w_up_n": I["L1_ffn_w_up"].astype(np.float32),
        "w_gate_n": I["L1_ffn_w_gate"].astype(np.float32),
        "w_down_n": I["L1_ffn_w_down"].astype(np.float32),
    }
    x0 = np.asarray(I["L0_attn_x"], np.float32)
    maps = []
    for c in range(8):
        m = dict(com)
        m["x_fm"] = np.ascontiguousarray(x0[2 * c:2 * c + 2].reshape(T, DM).T)
        maps.append(m)
    return maps


def kernel(**inputs):
    import numpy as np
    from concourse.bass_utils import run_bass_kernel_spmd
    nc = build({"phases": ["A0", "A1", "A2", "B", "D", "E", "F", "G", "H"],
                "taps": ["out"]})
    maps = host_in_maps(inputs)
    res = run_bass_kernel_spmd(nc, maps, core_ids=list(range(8)))
    full = np.empty((16, 2048, 1024), np.float32)
    for c in range(8):
        oc = np.asarray(res.results[c]["out"]).astype(np.float32)
        full[2 * c:2 * c + 2] = oc.T.reshape(2, 2048, 1024)
    return full
